# version 27
# baseline (speedup 1.0000x reference)
import numpy as np
import ml_dtypes
from contextlib import ExitStack

import concourse.bass as bass
import concourse.mybir as mybir
from concourse.bass_utils import run_bass_kernel_spmd

F32 = mybir.dt.float32
BF16 = mybir.dt.bfloat16
I32 = mybir.dt.int32
AF = mybir.ActivationFunctionType
OP = mybir.AluOpType
AX = mybir.AxisListType

MAGIC = 12582912.0
OFF = 256.0
CFIX = OFF + 0.5 - 2.0 ** -15
CLIP15 = 1.5 - 2.0 ** -22
NEG_BIG = -1.0e30
EPS = 1e-5

D = 1024
HID = 4096
ND = D // 128
NH = HID // 128
N_CORES = 8


class Prog:
    ENGS = ("sync", "gpsimd", "vector", "scalar", "tensor")
    AUTO = {"vector": "vv", "scalar": "sv"}

    def __init__(self):
        self.streams = {e: [] for e in self.ENGS}
        self.cnt = {}

    @staticmethod
    def _flat(waits):
        out = []
        for t in waits:
            if t is None:
                continue
            if isinstance(t, list):
                out.extend(x for x in t if x is not None)
            else:
                out.append(t)
        return out

    def op(self, eng, fn, waits=(), inc=None, inc_n=1):
        w = [(s, v) for (s, v) in self._flat(waits) if v > 0]
        if eng in self.AUTO:
            sem = self.AUTO[eng]
            assert inc is None or inc == sem
            idx = self.cnt.get(sem, 0) + 1
            self.cnt[sem] = idx
            chain = (sem, idx - 1)
            self.streams[eng].append((fn, w, sem, 1, chain))
            return (sem, idx)
        self.streams[eng].append((fn, w, inc, inc_n, None))
        if inc is not None:
            self.cnt[inc] = self.cnt.get(inc, 0) + inc_n
            return (inc, self.cnt[inc])
        return None

    def tok(self, sem):
        return (sem, self.cnt.get(sem, 0))

    def inc_last(self, eng, sem, n=1):
        fn, w, inc, inc_n, chain = self.streams[eng][-1]
        assert inc is None, "last op already has an inc"
        self.streams[eng][-1] = (fn, w, sem, n, chain)
        self.cnt[sem] = self.cnt.get(sem, 0) + n
        return (sem, self.cnt[sem])

    def emit(self, block, sems):
        for eng in self.ENGS:
            recs = self.streams[eng]

            def body(e, recs=recs, eng=eng):
                seen = {}
                for fn, waits, inc, inc_n, chain in recs:
                    for s, v in waits:
                        if v > seen.get(s, 0):
                            e.wait_ge(sems[s], v)
                            seen[s] = v
                    ins = fn(e)
                    if chain is not None and chain[1] > seen.get(chain[0], 0):
                        ins._wait_ge(sems[chain[0]], chain[1])
                        seen[chain[0]] = chain[1]
                    if inc is not None:
                        ins.then_inc(sems[inc], inc_n)
            getattr(block, eng)(body)


def build_program(T=2048, nseq=4, phases=5):
    NT = T // 128
    NBL = T // 512
    XQW = T + 2

    nc = bass.Bass("TRN2", target_bir_lowering=False)

    d_x = nc.dram_tensor("x", [nseq * T, D], F32, kind="ExternalInput").ap()
    d_y = nc.dram_tensor("y", [nseq * T, D], F32, kind="ExternalOutput").ap()
    d_wq = nc.dram_tensor("wq", [D, D], F32, kind="ExternalInput").ap()
    d_wk = nc.dram_tensor("wk", [D, D], F32, kind="ExternalInput").ap()
    d_wv = nc.dram_tensor("wv", [D, D], F32, kind="ExternalInput").ap()
    d_w1t = nc.dram_tensor("w1t", [D, HID], F32, kind="ExternalInput").ap()
    d_w2t = nc.dram_tensor("w2t", [HID, D], F32, kind="ExternalInput").ap()
    d_cw = nc.dram_tensor("cw", [128, NH * 3], F32, kind="ExternalInput").ap()
    d_cb = nc.dram_tensor("cb", [128, NH], F32, kind="ExternalInput").ap()
    d_id = nc.dram_tensor("identb", [128, 128], BF16, kind="ExternalInput").ap()
    d_e0 = nc.dram_tensor("e0b", [128, 128], BF16, kind="ExternalInput").ap()
    d_eq = nc.dram_tensor("eqb", [128, 512], BF16, kind="ExternalInput").ap()
    d_ones = nc.dram_tensor("onesb", [128, 1], BF16, kind="ExternalInput").ap()
    d_mask = nc.dram_tensor("maskf", [128, 128], F32, kind="ExternalInput").ap()
    d_wkb = nc.dram_tensor("wkb_s", [128, ND * D], BF16, kind="Internal").ap()
    d_wvb = nc.dram_tensor("wvb_s", [128, ND * D], BF16, kind="Internal").ap()
    d_w1q = nc.dram_tensor("w1q_s", [NH, 128, ND * 128], BF16, kind="Internal").ap()
    d_w2q = nc.dram_tensor("w2q_s", [NH, 128, ND * 128], BF16, kind="Internal").ap()
    d_x2 = nc.dram_tensor("x2_s", [T, D], F32, kind="Internal").ap()
    d_hs = nc.dram_tensor("hs_s", [NBL, NH, 128, 512], F32, kind="Internal").ap()
    d_sxh = nc.dram_tensor("sxh_s", [T], BF16, kind="Internal").ap()
    d_sxl = nc.dram_tensor("sxl_s", [T], BF16, kind="Internal").ap()
    d_shh = nc.dram_tensor("shh_s", [512], BF16, kind="Internal").ap()
    d_shl = nc.dram_tensor("shl_s", [512], BF16, kind="Internal").ap()

    es = ExitStack()

    def SB(name, shape, dt):
        return es.enter_context(nc.sbuf_tensor(name, shape, dt))

    identB = SB("identB", [128, 128], BF16)
    E0b = SB("E0b", [128, 128], BF16)
    EQb = SB("EQb", [128, 512], BF16)
    onesb = SB("onesb_sb", [128, 1], BF16)
    maskf = SB("maskf_sb", [128, 128], F32)
    convw = SB("convw", [128, NH * 3], F32)
    convb = SB("convb", [128, NH], F32)
    meanbc1 = SB("meanbc1", [128, 1], F32)
    meanbc2 = SB("meanbc2", [128, 1], F32)
    sq1bc = SB("sq1bc", [128, 1], F32)
    sq2bc = SB("sq2bc", [128, 1], F32)
    sxinvtm = SB("sxinvtm", [128, NT], F32)
    sxhi = SB("sxhi", [128, NT], BF16)
    sxlo = SB("sxlo", [128, NT], BF16)
    mrow = SB("mrow", [128, 1], F32)
    negm32 = SB("negm32", [128, 1], F32)
    lsum = SB("lsum", [128, 1], F32)
    linv = SB("linv", [128, 1], F32)
    st12 = [SB("st12a", [128, 12], F32), SB("st12b", [128, 12], F32)]
    mv = [SB("mva", [128, 2], F32), SB("mvb", [128, 2], F32)]
    varp = [SB("varpa", [128, 1], F32), SB("varpb", [128, 1], F32)]
    lnv = [SB("lnva", [128, 1], F32), SB("lnvb", [128, 1], F32)]
    rs = [SB("rsa", [128, 1], F32), SB("rsb", [128, 1], F32)]
    negmrs = [SB("negmrsa", [128, 1], F32), SB("negmrsb", [128, 1], F32)]
    amax = SB("amax", [128, 1], F32)
    arec = SB("arec", [128, 1], F32)
    scol = SB("scol", [128, 1], F32)
    matm = SB("matm", [128, 4], F32)
    strow = SB("strow", [128, 4], F32)
    sthi = SB("sthi", [128, 4], BF16)
    stlo = SB("stlo", [128, 4], BF16)
    shinvtm = SB("shinvtm", [128, 4], F32)
    scale2tm = SB("scale2tm", [128, 4], F32)
    psums1 = SB("psums1", [128, 16], F32)
    psums2 = SB("psums2", [128, 16], F32)
    redtmp = SB("redtmp", [128, 1], F32)
    redhi = SB("redhi", [128, 1], BF16)
    redlo = SB("redlo", [128, 1], BF16)
    sc11 = SB("sc11", [1, 1], F32)
    schi = SB("schi", [128, 1], BF16)
    sclo = SB("sclo", [128, 1], BF16)
    magicbc = SB("magicbc", [128, 1], F32)
    noffbc = SB("noffbc", [128, 1], F32)
    rstage_h = SB("rstage_h", [128, 512], BF16)
    rstage_l = SB("rstage_l", [128, 512], BF16)
    XQ2T = SB("XQ2T", [128, ND * XQW], BF16)
    sxinvb = SB("sxinvb", [128, T + 2], F32)
    WqB = SB("WqB", [128, ND * D], BF16)

    base = (nc.sbuf_base + 63) & ~63

    def AT(name, shape, dt, off):
        return nc.alloc_sbuf_tensor_at(name, shape, dt, offset=base + off)

    XTq = AT("XTq", [128, ND * T], BF16, 0)
    oL = ND * T * 2
    w1stage = [AT(f"w1s{k}", [128, ND * 128], BF16, oL + 2048 * k) for k in range(3)]
    hsf = [AT("hsf0", [128, 514], F32, oL + 6144),
           AT("hsf1", [128, 514], F32, oL + 8256)]
    ctmp = [AT("ctmp0", [128, 512], F32, oL + 10368),
            AT("ctmp1", [128, 512], F32, oL + 12416)]
    maxbufs = [AT(f"maxb{k}", [128, 512], F32, oL + 14464 + 2048 * k)
               for k in range(NBL)]
    negbufs = [AT(f"negb{k}", [128, 512], F32, oL + 14464 + 2048 * NBL + 2048 * k)
               for k in range(NBL)]
    assert 14464 + 4096 * NBL <= 32768
    Vtm = AT("Vtm", [128, NT * D], BF16, oL + 32768)
    o4 = oL + 32768 + NT * D * 2
    QT = AT("QT", [128, ND * 512], BF16, o4)
    Wv2 = AT("Wv2", [128, ND * D], BF16, o4)
    Pbuf = AT("Pbuf", [128, T], BF16, o4 + 8192)
    PT = AT("PT", [128, T], BF16, o4 + 8192 + T * 2)
    o5 = o4 + max(16384, 8192 + 2 * T * 2)
    xstage = [AT("xstage0", [128, D], F32, o5),
              AT("xstage1", [128, D], F32, o5 + 4096)]
    o6 = o5 + 8192
    t1buf = AT("t1buf", [128, D], F32, o6)
    x2buf = [AT("x2buf0", [128, D], F32, o6 + 4096),
             AT("x2buf1", [128, D], F32, o6 + 8192)]
    ytmp = AT("ytmp", [128, D], F32, o6 + 12288)
    i32t = AT("i32t", [128, D], I32, o6 + 16384)
    xq2 = AT("xq2", [128, D], BF16, o6 + 20480)
    WkB = AT("WkB", [128, ND * D], BF16, o6)
    xbstage = [AT("xbstage0", [128, D], BF16, o6 + 16384),
               AT("xbstage1", [128, D], BF16, o6 + 18432)]
    region_end_a = o6 + 22528

    oF = o4
    hsr = [AT(f"hsr{k}", [128, 512], F32, oF + 2048 * k) for k in range(3)]
    hsr.append(AT("hsr3", [128, 512], F32, oF + 31744))
    qtmp = [AT("qtmp0", [128, 512], F32, oF + 6144),
            AT("qtmp1", [128, 512], F32, oF + 8192),
            AT("qtmp2", [128, 512], F32, oF + 33792)]
    qi32 = [AT("qi32_0", [128, 512], I32, oF + 10240),
            AT("qi32_1", [128, 512], I32, oF + 12288),
            AT("qi32_2", [128, 512], I32, oF + 35840)]
    hqb = [AT("hq0", [128, 512], BF16, oF + 14336),
           AT("hq1", [128, 512], BF16, oF + 15360),
           AT("hq2", [128, 512], BF16, oF + 30720)]
    w2stage = [AT(f"w2s{k}", [128, ND * 128], BF16, oF + 16384 + 2048 * k)
               for k in range(4)]
    mxf = AT("mxf", [128, 512], F32, oF + 24576)
    mxhi = AT("mxhi", [128, 512], BF16, oF + 26624)
    mxlo = AT("mxlo", [128, 512], BF16, oF + 27648)
    sbc = AT("sbc", [128, 512], F32, oF + 28672)
    assert oF + 37888 <= region_end_a
    oV = oL + 32768 if NT * D * 2 >= 32768 else region_end_a
    hi2 = AT("hi2", [128, ND * 512], BF16, oV)
    lo2 = AT("lo2", [128, ND * 512], BF16, oV + 8192)
    x2rd = [AT("x2rd0", [128, D], F32, oV + 16384),
            AT("x2rd1", [128, D], F32, oV + 20480)]
    outb = [AT("outb0", [128, D], F32, oV + 24576),
            AT("outb1", [128, D], F32, oV + 28672)]
    region_end_a = max(region_end_a, oV + 32768)
    pstgh = [AT("pstgh0", [128, 2048], F32, oL),
             AT("pstgh1", [128, 2048], F32, oL + 8192)]
    pi32h = [AT("pi32h0", [128, 2048], I32, oL),
             AT("pi32h1", [128, 2048], I32, oL + 8192)]
    pternh = [AT("pternh0", [128, 2048], BF16, oL + 16384),
              AT("pternh1", [128, 2048], BF16, oL + 20480)]
    assert base + region_end_a <= nc.sbuf_top, (
        f"SBUF overflow: {base}+{region_end_a} > {nc.sbuf_top}")

    ps = es.enter_context(nc.psum_tensor("ps", [128, 4096], F32))

    NLANE = 4
    sem_names = (["tv", "vv", "sv"]
                 + [f"ld{i}" for i in range(NLANE)]
                 + [f"gd{i}" for i in range(NLANE)])
    sems = {n: es.enter_context(nc.semaphore(n)) for n in sem_names}
    P = Prog()
    _rr = {"ld": 0, "gd": 0}

    def _dma(eng, queue, fn, waits):
        lane = f"{queue}{_rr[queue] % NLANE}"
        _rr[queue] += 1
        w = list(waits) + [P.tok(lane)]
        return P.op(eng, fn, w, lane, 16)

    def Sy(fn, waits=()):
        return _dma("sync", "ld", fn, waits)

    def Gp(fn, waits=()):
        return _dma("gpsimd", "gd", fn, waits)

    def Qtok(queue):
        return [P.tok(f"{queue}{i}") for i in range(NLANE)]

    def Tn(fn, waits=(), inc=None):
        return P.op("tensor", fn, waits, inc, 1)

    def Vc(fn, waits=(), inc=None):
        return P.op("vector", fn, waits, inc, 1)

    def Sc(fn, waits=(), inc=None):
        return P.op("scalar", fn, waits, inc, 1)

    def XQ3(cd_count=ND):
        return XQ2T[:, :].rearrange("p (c w) -> p c w", w=XQW)

    Sy(lambda e: e.dma_start(identB[:, :], d_id))
    Sy(lambda e: e.dma_start(E0b[:, :], d_e0))
    Sy(lambda e: e.dma_start(EQb[:, :], d_eq))
    Sy(lambda e: e.dma_start(onesb[:, :], d_ones))
    Sy(lambda e: e.dma_start(maskf[:, :], d_mask))
    Sy(lambda e: e.dma_start(convw[:, :], d_cw))
    Sy(lambda e: e.dma_start(convb[:, :], d_cb))
    t_const = Qtok("ld")
    Vc(lambda e: e.memset(XQ3()[:, :, 0], 0.0), waits=[])
    Vc(lambda e: e.memset(XQ3()[:, :, XQW - 1], 0.0))
    Vc(lambda e: e.memset(sxinvb[:, 0:1], 0.0))
    Vc(lambda e: e.memset(sxinvb[:, T + 1:T + 2], 0.0))
    Vc(lambda e: e.memset(rstage_h[:, :], 0.0))
    Vc(lambda e: e.memset(rstage_l[:, :], 0.0))
    Vc(lambda e: e.memset(magicbc[:, :], MAGIC))
    Vc(lambda e: e.memset(noffbc[:, :], -OFF))
    Vc(lambda e: e.memset(schi[:, :], 0.0))
    Vc(lambda e: e.memset(sclo[:, :], 0.0), inc="vv")
    t_guards = P.tok("vv")

    def _w3(dap):
        return dap.rearrange("(c p) e -> p c e", p=128)

    def _s3(sap):
        return sap.rearrange("p (c e) -> p c e", e=D)

    tg_wq = Gp(lambda e: e.dma_start(_s3(WkB[:, :]), _w3(d_wq)))
    tg_wk = Gp(lambda e: e.dma_start(_s3(Wv2[:, :]), _w3(d_wk)))
    t_mev = {}
    for i_m in range(ND):
        for jh in range(2):
            wmb = [tg_wq, tg_wk]
            if (i_m - 1, jh) in t_mev:
                wmb.append(t_mev[(i_m - 1, jh)])
            for ce in range(ND):
                Tn(lambda e, i_m=i_m, jh=jh, ce=ce: e.matmul(
                    ps[:, 2048 + jh * 512: 2048 + (jh + 1) * 512],
                    WkB[:, ce * D + i_m * 128: ce * D + i_m * 128 + 128],
                    Wv2[:, ce * D + jh * 512: ce * D + (jh + 1) * 512],
                    start=(ce == 0), stop=(ce == ND - 1)),
                    waits=wmb if ce == 0 else ())
            tmb = P.inc_last("tensor", "tv")
            t_mev[(i_m, jh)] = Vc(lambda e, i_m=i_m, jh=jh: e.tensor_copy(
                WqB[:, i_m * D + jh * 512: i_m * D + (jh + 1) * 512],
                ps[:, 2048 + jh * 512: 2048 + (jh + 1) * 512]), waits=[tmb])
    P.op("vector", lambda e: e.nop(), inc="vv")
    t_mdone = P.tok("vv")
    tg_wv = Gp(lambda e: e.dma_start(_s3(WkB[:, :]), _w3(d_wv)),
               waits=[P.tok("tv")])
    t_wvst = Gp(lambda e: e.dma_start(d_wvb, WkB[:, :]), waits=[tg_wv])

    def _ld1(e, hk, st):
        cd, k = hk // 2, hk % 2
        return e.dma_start(st[:, :],
                           d_w1t[cd * 128:(cd + 1) * 128, k * 2048:(k + 1) * 2048])

    def _ld2(e, fk, st):
        r0 = fk * 256
        return e.dma_start(st[:, :].rearrange("p (g e) -> p g e", e=D),
                           d_w2t[r0:r0 + 256, :].rearrange("(g p) e -> p g e", p=128))

    pro_steps = []
    tql = {}

    def _p1w1(hk):
        st = pstgh[hk % 2]
        w = [tql[hk - 2]] if hk >= 2 else []
        tld = Sy(lambda e, hk=hk, st=st: _ld1(e, hk, st), waits=w)
        tql[hk] = Sc(lambda e, hk=hk, st=st: e.activation(
            st[:, :], st[:, :], AF.Abs, accum_out=psums1[:, hk:hk + 1]),
            waits=[tld])

    def _p1w2(fk):
        st = pstgh[fk % 2]
        tld = Gp(lambda e, fk=fk, st=st: _ld2(e, fk, st),
                 waits=[tql[16 + fk - 2]] if fk >= 2 else [tql[15]])
        tql[16 + fk] = Sc(lambda e, fk=fk, st=st: e.activation(
            st[:, :], st[:, :], AF.Abs, accum_out=psums2[:, fk:fk + 1]),
            waits=[tld])

    for _hk in range(16):
        pro_steps.append(lambda hk=_hk: _p1w1(hk))
    for _fk in range(16):
        pro_steps.append(lambda fk=_fk: _p1w2(fk))

    def mean_pipeline(psumsbuf, n, scale, meanbc, sqbc, w0=()):
        Vc(lambda e: e.tensor_reduce(redtmp[:, :], psumsbuf[:, 0:n], AX.X, OP.add, False),
           waits=w0)
        Vc(lambda e: e.tensor_copy(redhi[:, :], redtmp[:, :]))
        Vc(lambda e: e.scalar_tensor_tensor(redlo[:, :], redhi[:, :], -1.0, redtmp[:, :],
                                            OP.mult, OP.add), inc="vv")
        tv1 = P.tok("vv")
        Tn(lambda e: e.matmul(ps[0:1, 3072:3073], onesb[:, :], redhi[:, :],
                              start=True, stop=False), waits=[tv1, t_const])
        Tn(lambda e: e.matmul(ps[0:1, 3072:3073], onesb[:, :], redlo[:, :],
                              start=False, stop=True), inc="tv")
        tt1 = P.tok("tv")
        Vc(lambda e: e.tensor_scalar(sc11[:, :], ps[0:1, 3072:3073], scale, 1e-5,
                                     OP.mult, OP.max), waits=[tt1])
        Vc(lambda e: e.tensor_copy(schi[0:1, :], sc11[:, :]), waits=[t_guards])
        Vc(lambda e: e.scalar_tensor_tensor(sclo[0:1, :], schi[0:1, :], -1.0, sc11[:, :],
                                            OP.mult, OP.add), inc="vv")
        tv2 = P.tok("vv")
        Tn(lambda e: e.matmul(ps[:, 3584:3585], E0b[:, :], schi[:, :],
                              start=True, stop=False), waits=[tv2])
        Tn(lambda e: e.matmul(ps[:, 3584:3585], E0b[:, :], sclo[:, :],
                              start=False, stop=True), inc="tv")
        tt2 = P.tok("tv")
        Vc(lambda e: e.tensor_copy(meanbc[:, :], ps[:, 3584:3585]), waits=[tt2])
        Vc(lambda e: e.reciprocal(sqbc[:, :], meanbc[:, :]), inc="vv")

    tok_box = {}

    def _means_all():
        mean_pipeline(psums1, 16, 1.0 / (D * HID), meanbc1, sq1bc, w0=[tql[31]])
        mean_pipeline(psums2, 16, 1.0 / (D * HID), meanbc2, sq2bc, w0=[tql[31]])
        tok_box["means"] = P.tok("vv")

    pro_steps.append(_means_all)
    MEANS_STEP = len(pro_steps)

    def ternarize_h(par, sqbc, tld, tst_prev):
        st, pi, pt = pstgh[par], pi32h[par], pternh[par]
        Vc(lambda e, st=st: e.tensor_scalar(st[:, :], st[:, :], sqbc[:, 0:1], CLIP15,
                                            OP.mult, OP.min), waits=[tld])
        Vc(lambda e, st=st: e.tensor_scalar(st[:, :], st[:, :], -CLIP15, MAGIC,
                                            OP.max, OP.add))
        tv3 = Vc(lambda e, st=st, pi=pi: e.tensor_scalar(pi[:, :], st[:, :], -MAGIC, CFIX,
                                                         OP.add, OP.add))
        tc = Sc(lambda e, pi=pi, pt=pt: e.activation(
            pt[:, :], pi[:, :], AF.Identity, bias=noffbc[:, 0:1]),
            waits=[tv3, tst_prev])
        return tv3, tc

    tqs = {}

    def _p2w2(fk):
        par = fk % 2
        w = [tok_box["means"]]
        if fk >= 2:
            w.extend(tqs[fk - 2][:2])
        tld = Gp(lambda e, fk=fk, par=par: _ld2(e, fk, pstgh[par]), waits=w)
        tv3, tc = ternarize_h(par, sq2bc, tld,
                              tqs[fk - 2][2] if fk >= 2 else None)
        tst = Gp(lambda e, fk=fk, par=par: e.dma_start(
            d_w2q[2 * fk:2 * fk + 2].transpose([1, 0, 2]),
            pternh[par][:, :].rearrange("p (g e) -> p g e", e=D)), waits=[tc])
        tqs[fk] = (tv3, tc, tst)

    def _p2w1(hk):
        par = hk % 2
        gk = 16 + hk
        w = list(tqs[gk - 2][:2])
        tld = Sy(lambda e, hk=hk, par=par: _ld1(e, hk, pstgh[par]), waits=w)
        tv3, tc = ternarize_h(par, sq1bc, tld, tqs[gk - 2][2])
        cd, k = hk // 2, hk % 2
        tst = Gp(lambda e, cd=cd, k=k, par=par: e.dma_start(
            d_w1q[k * 16:(k + 1) * 16, :, cd * 128:(cd + 1) * 128].transpose([1, 0, 2]),
            pternh[par][:, :].rearrange("p (c h) -> p c h", h=128)), waits=[tc])
        tqs[gk] = (tv3, tc, tst)

    for _fk in range(16):
        pro_steps.append(lambda fk=_fk: _p2w2(fk))
    for _hk in range(16):
        pro_steps.append(lambda hk=_hk: _p2w1(hk))

    pro_idx = [0]

    def pro_pump(n):
        k = 0
        while pro_idx[0] < len(pro_steps) and k < n:
            pro_steps[pro_idx[0]]()
            pro_idx[0] += 1
            k += 1

    def pro_drain(upto=None):
        tgt = len(pro_steps) if upto is None else upto
        while pro_idx[0] < tgt:
            pro_steps[pro_idx[0]]()
            pro_idx[0] += 1

    t_prologue_g = None

    PUMP_Q, PUMP_S, PUMP_TILE, PUMP_BC = 1, 4, 6, 2
    ffn1_ready = []
    unlock_tok = {}
    gck = [0]
    g_hmm, g_mainev, g_conv, g_tree, g_spill = {}, {}, {}, {}, {}
    g_evLR = {}
    g_sil = {}
    t_spill_h = {}
    t_treeblk = {}

    def emit_chunk(jj, c, dummies):
        gc = gck[0]
        gck[0] += 1
        t0 = jj * 512
        stw = w1stage[gc % 3]
        hsb = hsf[gc % 2]
        ct = ctmp[gc % 2]
        wld = [g_hmm[gc - 3]] if gc >= 3 else [t_prologue_g]
        tld = Sy(lambda e, c=c, stw=stw: e.dma_start(stw[:, :], d_w1q[c]), waits=wld)
        ul = unlock_tok[jj]
        whL = [tld, ul, g_evLR.get(gc - 1)]
        for cd in range(ND):
            Tn(lambda e, cd=cd, stw=stw, t0=t0: e.matmul(
                ps[:, 3072:3073], stw[:, cd * 128:(cd + 1) * 128],
                XQ2T[:, cd * XQW + t0: cd * XQW + t0 + 1],
                start=(cd == 0), stop=(cd == ND - 1)), waits=whL if cd == 0 else ())
        for cd in range(ND):
            Tn(lambda e, cd=cd, stw=stw, t0=t0: e.matmul(
                ps[:, 3073:3074], stw[:, cd * 128:(cd + 1) * 128],
                XQ2T[:, cd * XQW + t0 + 513: cd * XQW + t0 + 514],
                start=(cd == 0), stop=(cd == ND - 1)))
        t_hLR = P.inc_last("tensor", "tv")
        wmn = [g_mainev.get(gc - 1)] if gc >= 1 else [tld, ul]
        for cd in range(ND):
            Tn(lambda e, cd=cd, stw=stw, t0=t0: e.matmul(
                ps[:, 3584:4096], stw[:, cd * 128:(cd + 1) * 128],
                XQ2T[:, cd * XQW + 1 + t0: cd * XQW + 1 + t0 + 512],
                start=(cd == 0), stop=(cd == ND - 1)),
                waits=wmn if cd == 0 else ())
        g_hmm[gc] = P.inc_last("tensor", "tv")
        Vc(lambda e, hsb=hsb, t0=t0: e.tensor_tensor(
            hsb[:, 0:1], ps[:, 3072:3073], sxinvb[:, t0:t0 + 1], OP.mult),
            waits=[t_hLR, ul, g_conv.get(gc - 2)])
        g_evLR[gc] = Vc(lambda e, hsb=hsb, t0=t0: e.tensor_tensor(
            hsb[:, 513:514], ps[:, 3073:3074], sxinvb[:, t0 + 513:t0 + 514],
            OP.mult))
        if dummies:
            Tn(lambda e: e.matmul(ps[0:64, 0:64], identB[:, 0:64], identB[:, 0:64],
                                  start=True, stop=True))
            Tn(lambda e: e.matmul(ps[0:64, 0:64], identB[:, 0:64], identB[:, 0:64],
                                  start=True, stop=True))
        g_mainev[gc] = Vc(lambda e, hsb=hsb, t0=t0: e.tensor_tensor(
            hsb[:, 1:513], ps[:, 3584:4096], sxinvb[:, 1 + t0:1 + t0 + 512],
            OP.mult), waits=[g_hmm[gc]])
        return (jj, c, gc)

    def chunk_mid(st):
        jj, c, gc = st
        hsb = hsf[gc % 2]
        ct = ctmp[gc % 2]
        t_tap = Sc(lambda e, ct=ct, hsb=hsb, c=c: e.activation(
            ct[:, :], hsb[:, 0:512], AF.Copy, scale=convw[:, 3 * c:3 * c + 1]),
            waits=[g_mainev[gc], g_spill.get(gc - 2), g_tree.get(gc - 2)])
        Vc(lambda e, ct=ct, hsb=hsb, c=c: e.scalar_tensor_tensor(
            ct[:, :], hsb[:, 1:513], convw[:, 3 * c + 1:3 * c + 2], ct[:, :],
            OP.mult, OP.add), waits=[t_tap])
        g_conv[gc] = Vc(lambda e, ct=ct, hsb=hsb, c=c: e.scalar_tensor_tensor(
            ct[:, :], hsb[:, 2:514], convw[:, 3 * c + 2:3 * c + 3], ct[:, :],
            OP.mult, OP.add))
        g_sil[gc] = Sc(lambda e, ct=ct, c=c: e.activation(
            ct[:, :], ct[:, :], AF.Silu, bias=convb[:, c:c + 1]),
            waits=[g_conv[gc]])

    def chunk_back(st):
        jj, c, gc = st
        ct = ctmp[gc % 2]
        t_sil = g_sil[gc]
        g_spill[gc] = t_spill_h[(jj, c)] = Gp(
            lambda e, jj=jj, c=c, ct=ct: e.dma_start(d_hs[jj, c], ct[:, :]),
            waits=[t_sil])
        mb, nb = maxbufs[jj], negbufs[jj]
        if c == 0:
            Vc(lambda e, mb=mb, ct=ct: e.tensor_copy(mb[:, :], ct[:, :]),
               waits=[t_sil, t_prologue_g])
            g_tree[gc] = t_treeblk[jj] = Vc(
                lambda e, nb=nb, ct=ct: e.tensor_scalar(
                    nb[:, :], ct[:, :], -1.0, None, OP.mult))
        else:
            Vc(lambda e, mb=mb, ct=ct: e.tensor_tensor(
                mb[:, :], mb[:, :], ct[:, :], OP.max), waits=[t_sil])
            g_tree[gc] = t_treeblk[jj] = Vc(
                lambda e, nb=nb, ct=ct: e.scalar_tensor_tensor(
                    nb[:, :], ct[:, :], -1.0, nb[:, :], OP.mult, OP.max))

    fstates = []
    stage_idx = [0, 0]

    def pump(n, dummies=False):
        k = 0
        while ffn1_ready and k < n:
            jj, c = ffn1_ready.pop(0)
            fstates.append(emit_chunk(jj, c, dummies))
            if stage_idx[0] < len(fstates) - 1:
                chunk_mid(fstates[stage_idx[0]])
                stage_idx[0] += 1
            if stage_idx[1] < stage_idx[0] - 1:
                chunk_back(fstates[stage_idx[1]])
                stage_idx[1] += 1
            k += 1

    def pump_drain():
        while stage_idx[0] < len(fstates) or stage_idx[1] < len(fstates):
            if stage_idx[0] < len(fstates):
                chunk_mid(fstates[stage_idx[0]])
                stage_idx[0] += 1
            if stage_idx[1] < stage_idx[0]:
                chunk_back(fstates[stage_idx[1]])
                stage_idx[1] += 1

    KJ = 512

    for s in range(nseq if phases >= 2 else 0):
        row0 = s * T
        t_prev_t = P.tok("tv")
        t_prev_v = P.tok("vv")
        t_prev_s = P.tok("sv")
        t_prev_g = Qtok("gd")

        t_cast = {}
        t_evac = {}
        for i in range(NT):
            stx = xstage[i % 2]
            stb = xbstage[i % 2]
            if i >= 2:
                w = [t_cast[i - 2]]
            elif s == 0:
                w = []
            else:
                w = [t_prev_v, t_prev_s, t_prev_g]
            tld = Sy(lambda e, i=i, stx=stx, row0=row0: e.dma_start(
                stx[:, :], d_x[row0 + i * 128: row0 + (i + 1) * 128, :]), waits=w)
            wc = [tld]
            if i >= 2:
                wc.append(t_evac[i - 2])
            t_cast[i] = Vc(lambda e, stx=stx, stb=stb: e.tensor_copy(stb[:, :], stx[:, :]),
                           waits=wc, inc="vv")
            pc = (i % 2) * 1024
            wts = [t_cast[i], t_guards]
            if i >= 2:
                wts.append(t_evac[i - 2])
            for cd in range(ND):
                Tn(lambda e, cd=cd, pc=pc, stb=stb: e.matmul(
                    ps[:, pc + cd * 128: pc + (cd + 1) * 128],
                    stb[:, cd * 128:(cd + 1) * 128], identB[:, :],
                    start=True, stop=True), waits=wts if cd == 0 else ())
            tt = P.inc_last("tensor", "tv")
            t_evac[i] = Vc(lambda e, i=i, pc=pc: e.tensor_copy(
                XTq[:, :].rearrange("p (c t) -> p c t", t=T)[:, :, i * 128:(i + 1) * 128],
                ps[:, pc:pc + 1024].rearrange("p (c t) -> p c t", t=128)),
                waits=[tt], inc="vv")
            pro_pump(1)
        t_xtq = P.tok("vv")

        if phases < 3:
            continue
        tldv = Sy(lambda e: e.dma_start(Wv2[:, :], d_wvb),
                  waits=([t_mdone, t_wvst] if s == 0 else
                         [t_prev_t, t_prev_g, t_mdone]))
        t_kdone = P.tok("sv")
        tve = {}
        for i in range(NT):
            pc = (i % 2) * 1024
            wts = [tldv, t_xtq]
            if i >= 2:
                wts.append(tve[i - 2])
            else:
                wts.append(t_evac[NT - 1])
            for cd in range(ND):
                for h in range(2):
                    Tn(lambda e, i=i, cd=cd, h=h, pc=pc: e.matmul(
                        ps[:, pc + h * 512: pc + (h + 1) * 512],
                        XTq[:, cd * T + i * 128: cd * T + (i + 1) * 128],
                        Wv2[:, cd * D + h * 512: cd * D + (h + 1) * 512],
                        start=(cd == 0), stop=(cd == ND - 1)),
                        waits=wts if (cd == 0 and h == 0) else ())
            tt = P.inc_last("tensor", "tv")
            tve[i] = Sc(lambda e, i=i, pc=pc: e.copy(
                Vtm[:, i * D:(i + 1) * D], ps[:, pc:pc + 1024]), waits=[tt])
            pro_pump(2)
        t_vdone = P.tok("sv")

        if phases < 4:
            continue
        w45 = [t_mdone, t_prev_s, t_prev_v]
        q03 = {k: None for k in range(4)}
        t_ptevac = None
        t_bcast_evac = None
        t_exp = {}
        t_stt = {}
        t_spill = {}
        pend = None
        t_xq2mm = None

        def flush_pend():
            nonlocal pend, w45, t_xq2mm
            if pend is None:
                return
            ip, tvq = pend
            pend = None
            wq2 = [tvq] + w45
            for cd in range(ND):
                Tn(lambda e, cd=cd: e.matmul(
                    ps[:, 2048 + cd * 128: 2048 + (cd + 1) * 128],
                    xq2[:, cd * 128:(cd + 1) * 128], identB[:, :],
                    start=True, stop=True), waits=wq2 if cd == 0 else ())
            tt = P.inc_last("tensor", "tv")
            t_xq2mm = tt
            ev = Sc(lambda e, ip=ip: e.copy(
                XQ3()[:, :, 1 + ip * 128: 1 + (ip + 1) * 128],
                ps[:, 2048:3072].rearrange("p (c t) -> p c t", t=128)),
                waits=[tt])
            w45 = [ev]

        for sb in range(NBL):
            for ce in range(ND):
                wts = [t_vdone, t_kdone, q03[ce % 4]]
                if sb > 0:
                    wts.append(t_exp[sb * 4 - 1])
                pc = (ce % 4) * 512
                for cd in range(ND):
                    Tn(lambda e, ce=ce, cd=cd, sb=sb, pc=pc: e.matmul(
                        ps[:, pc:pc + 512],
                        WqB[:, cd * D + ce * 128: cd * D + ce * 128 + 128],
                        XTq[:, cd * T + sb * 512: cd * T + (sb + 1) * 512],
                        start=(cd == 0), stop=(cd == ND - 1)),
                        waits=wts if cd == 0 else ())
                tt = P.inc_last("tensor", "tv")
                q03[ce % 4] = Sc(lambda e, ce=ce, pc=pc: e.copy(
                    QT[:, ce * 512:(ce + 1) * 512], ps[:, pc:pc + 512]),
                    waits=[tt])
                pro_pump(1)
                if phases >= 5:
                    pump(PUMP_Q)
            t_qslab = P.tok("sv")

            for iq in range(4):
                i = sb * 4 + iq
                L = 128 * (i + 1)
                nkb = (L + KJ - 1) // KJ
                wts = [t_qslab]
                if i > 0:
                    wts.append(t_exp[i - 1])
                for ce in range(ND):
                    for kb in range(nkb):
                        Jk = min(KJ, L - kb * KJ)
                        Tn(lambda e, ce=ce, kb=kb, Jk=Jk, iq=iq: e.matmul(
                            ps[:, kb * KJ: kb * KJ + Jk],
                            QT[:, ce * 512 + iq * 128: ce * 512 + (iq + 1) * 128],
                            XTq[:, ce * T + kb * KJ: ce * T + kb * KJ + Jk],
                            start=(ce == 0), stop=(ce == ND - 1)),
                            waits=wts if (ce == 0 and kb == 0) else ())
                t_s = P.inc_last("tensor", "tv")
                pro_pump(1)
                if phases >= 5 and iq % 2 == 1:
                    pump(PUMP_S)
                flush_pend()
                Vc(lambda e, L=L: e.tensor_tensor(
                    ps[:, L - 128:L], ps[:, L - 128:L], maskf[:, :], OP.add), waits=[t_s])
                Vc(lambda e, L=L: e.tensor_reduce(
                    mrow[:, :], ps[:, 0:L], AX.X, OP.max, False))
                tv_m = Vc(lambda e: e.tensor_scalar(
                    negm32[:, :], mrow[:, :], -1.0 / 32.0, None, OP.mult))
                t_exp[i] = Sc(lambda e, L=L: e.activation(
                    Pbuf[:, 0:L], ps[:, 0:L], AF.Exp, bias=negm32[:, 0:1],
                    scale=1.0 / 32.0, accum_out=lsum[:, 0:1]),
                    waits=[tv_m])
                tv_li = Vc(lambda e: e.reciprocal(linv[:, :], lsum[:, :]),
                           waits=[t_exp[i]])
                for g in range(0, i + 1, 8):
                    cnt = min(8, i + 1 - g)
                    wpt = [t_exp[i]] + w45
                    for c in range(cnt):
                        Tn(lambda e, g=g, c=c: e.matmul(
                            ps[:, 2048 + c * 128: 2048 + (c + 1) * 128],
                            Pbuf[:, (g + c) * 128:(g + c + 1) * 128], identB[:, :],
                            start=True, stop=True), waits=wpt if c == 0 else ())
                    tt = P.inc_last("tensor", "tv")
                    ev = Sc(lambda e, g=g, cnt=cnt: e.copy(
                        PT[:, g * 128:(g + cnt) * 128],
                        ps[:, 2048:2048 + cnt * 128]), waits=[tt])
                    w45 = [ev]
                t_ptevac = w45[0]
                watt = [t_ptevac]
                if i > 0:
                    watt.append(t_stt[i - 1])
                for c in range(i + 1):
                    for h in range(2):
                        Tn(lambda e, c=c, h=h, i=i: e.matmul(
                            ps[:, 2048 + h * 512: 2048 + (h + 1) * 512],
                            PT[:, c * 128:(c + 1) * 128],
                            Vtm[:, c * D + h * 512: c * D + (h + 1) * 512],
                            start=(c == 0), stop=(c == i)),
                            waits=watt if (c == 0 and h == 0) else ())
                t_att = P.inc_last("tensor", "tv")
                stx = xstage[i % 2]
                tldr = Sy(lambda e, i=i, stx=stx, row0=row0: e.dma_start(
                    stx[:, :], d_x[row0 + i * 128: row0 + (i + 1) * 128, :]),
                    waits=[t_stt[i - 2]] if i >= 2 else [t_xtq])
                pr = i % 2
                wkbg = [t_wvst] if (s == 0 and i < 2) else []
                t_stt[i] = Vc(lambda e, stx=stx: e.scalar_tensor_tensor(
                    t1buf[:, :], ps[:, 2048:3072], linv[:, 0:1], stx[:, :],
                    OP.mult, OP.add), waits=[t_att, tldr, tv_li] + wkbg)
                w45 = [t_stt[i]]
                Vc(lambda e, pr=pr: e.bn_stats(st12[pr][:, 0:6], t1buf[:, 0:512]))
                Vc(lambda e, pr=pr: e.bn_stats(st12[pr][:, 6:12], t1buf[:, 512:1024]))
                Vc(lambda e, pr=pr: e.bn_aggr(mv[pr][:, :], st12[pr][:, :]))
                tv_v = Vc(lambda e, pr=pr: e.tensor_scalar(
                    varp[pr][:, :], mv[pr][:, 1:2], EPS, None, OP.add))
                Sc(lambda e, pr=pr: e.activation(lnv[pr][:, :], varp[pr][:, :], AF.Ln),
                   waits=[tv_v])
                t_rs = Sc(lambda e, pr=pr: e.activation(
                    rs[pr][:, :], lnv[pr][:, :], AF.Exp, scale=-0.5))
                tv_nm = Vc(lambda e, pr=pr: e.scalar_tensor_tensor(
                    negmrs[pr][:, :], mv[pr][:, 0:1], -1.0, rs[pr][:, :],
                    OP.mult, OP.mult), waits=[t_rs])
                xb2 = x2buf[i % 2]
                wx2 = [tv_nm] + wkbg
                if i >= 2:
                    wx2.append(t_spill[i - 2])
                t_x2 = Sc(lambda e, xb2=xb2, pr=pr: e.activation(
                    xb2[:, :], t1buf[:, :], AF.Identity, bias=negmrs[pr][:, 0:1],
                    scale=rs[pr][:, 0:1]), waits=wx2)
                t_spill[i] = Gp(lambda e, i=i, xb2=xb2: e.dma_start(
                    d_x2[i * 128:(i + 1) * 128, :], xb2[:, :]), waits=[t_x2])
                Vc(lambda e, xb2=xb2: e.tensor_reduce(
                    amax[:, :], xb2[:, :], AX.X, OP.max, True), waits=[t_x2])
                Vc(lambda e: e.tensor_scalar(amax[:, :], amax[:, :], 1e-5, None, OP.max))
                Vc(lambda e: e.reciprocal(arec[:, :], amax[:, :]))
                Vc(lambda e: e.tensor_scalar(scol[:, :], arec[:, :], 127.0, None, OP.mult))
                Vc(lambda e, i=i: e.reciprocal(sxinvtm[:, i:i + 1], scol[:, :]))
                Vc(lambda e, xb2=xb2: e.tensor_scalar(
                    ytmp[:, :], xb2[:, :], scol[:, 0:1], MAGIC, OP.mult, OP.add),
                   waits=wkbg)
                tv_i32 = Vc(lambda e: e.tensor_scalar(i32t[:, :], ytmp[:, :], -MAGIC,
                                                      CFIX, OP.add, OP.add))
                tv_q = Sc(lambda e: e.activation(
                    xq2[:, :], i32t[:, :], AF.Identity, bias=noffbc[:, 0:1]),
                    waits=[t_xq2mm, tv_i32])
                pend = (i, tv_q)
                pro_pump(1)
                if phases >= 5 and iq % 2 == 1:
                    pump(PUMP_TILE)
            flush_pend()
            if s == 0:
                pro_drain(MEANS_STEP)
            t_sxh = Sc(lambda e, sb=sb: e.copy(
                sxhi[:, sb * 4:(sb + 1) * 4], sxinvtm[:, sb * 4:(sb + 1) * 4]),
                waits=[P.tok("vv")])
            tv_l = Vc(lambda e, sb=sb: e.scalar_tensor_tensor(
                sxlo[:, sb * 4:(sb + 1) * 4], sxhi[:, sb * 4:(sb + 1) * 4], -1.0,
                sxinvtm[:, sb * 4:(sb + 1) * 4], OP.mult, OP.add), waits=[t_sxh])
            wtr1 = [tv_l] + w45
            Tn(lambda e, sb=sb: e.matmul(ps[0:4, 2048:2176], sxhi[:, sb * 4:(sb + 1) * 4],
                                         identB[:, :], start=True, stop=False),
               waits=wtr1)
            Tn(lambda e, sb=sb: e.matmul(ps[0:4, 2048:2176], sxlo[:, sb * 4:(sb + 1) * 4],
                                         identB[:, :], start=False, stop=True), inc="tv")
            tt_t = P.tok("tv")
            t_rh = Sc(lambda e: e.copy(rstage_h[0:4, 0:128], ps[0:4, 2048:2176]),
                      waits=[tt_t])
            t_rl = Vc(lambda e: e.scalar_tensor_tensor(
                rstage_l[0:4, 0:128], rstage_h[0:4, 0:128], -1.0, ps[0:4, 2048:2176],
                OP.mult, OP.add), waits=[t_rh])
            for q in range(4):
                Tn(lambda e, q=q: e.matmul(
                    ps[:, 2048 + q * 128: 2048 + (q + 1) * 128],
                    EQb[:, q * 128:(q + 1) * 128], rstage_h[:, 0:128],
                    start=True, stop=False), waits=[t_rl] if q == 0 else ())
                Tn(lambda e, q=q: e.matmul(
                    ps[:, 2048 + q * 128: 2048 + (q + 1) * 128],
                    EQb[:, q * 128:(q + 1) * 128], rstage_l[:, 0:128],
                    start=False, stop=True))
            P.inc_last("tensor", "tv")
            ttb = P.tok("tv")
            t_bcast_evac = Vc(lambda e, sb=sb: e.tensor_scalar(
                sxinvb[:, 1 + sb * 512: 1 + (sb + 1) * 512], ps[:, 2048:2560],
                meanbc1[:, 0:1], None, OP.mult), waits=[ttb, tok_box["means"]])
            w45 = [t_bcast_evac]
            pro_pump(1)
            if phases >= 5:
                newly = []
                if sb >= 1:
                    newly.append(sb - 1)
                if sb == NBL - 1:
                    newly.append(NBL - 1)
                if newly:
                    pro_drain()
                    if t_prologue_g is None:
                        t_prologue_g = Qtok("gd")
                    for jj in newly:
                        unlock_tok[jj] = t_bcast_evac
                        ffn1_ready.extend((jj, c) for c in range(NH))
                pump(PUMP_BC)
        t_attn_v = P.tok("vv")
        t_attn_s = P.tok("sv")
        t_attn_t = P.tok("tv")

        if phases < 5:
            continue
        pump(10 ** 9, dummies=True)
        pump_drain()
        assert not ffn1_ready
        t_tt2 = {}
        t_sm2 = {}
        t_i322 = {}
        t_hq = {}
        t_ldh = {}
        t_f2mm = {}
        t_y2T_ev = {}
        t_ln2 = {}
        t_outs = {}
        for j in range(NBL):
            Vc(lambda e, j=j: e.tensor_tensor(
                mxf[:, :], maxbufs[j][:, :], negbufs[j][:, :], OP.max),
                waits=[t_treeblk[j], t_attn_t])
            t_mxhi = Sc(lambda e: e.copy(mxhi[:, :], mxf[:, :]),
                        waits=[P.tok("vv")])
            tv_ml = Vc(lambda e: e.scalar_tensor_tensor(
                mxlo[:, :], mxhi[:, :], -1.0, mxf[:, :], OP.mult, OP.add),
                waits=[t_mxhi])
            wtr = [tv_ml, P.tok("tv"), P.tok("vv"), P.tok("sv")]
            for q in range(4):
                Tn(lambda e, q=q: e.matmul(
                    ps[:, 3584 + q * 128:3584 + (q + 1) * 128],
                    mxhi[:, q * 128:(q + 1) * 128], identB[:, :], start=True, stop=False),
                    waits=wtr if q == 0 else ())
                Tn(lambda e, q=q: e.matmul(
                    ps[:, 3584 + q * 128:3584 + (q + 1) * 128],
                    mxlo[:, q * 128:(q + 1) * 128], identB[:, :], start=False, stop=True))
            tt_tr = P.inc_last("tensor", "tv")
            for q in range(4):
                Vc(lambda e, q=q: e.tensor_reduce(
                    matm[:, q:q + 1], ps[:, 3584 + q * 128:3584 + (q + 1) * 128],
                    AX.X, OP.max, False), waits=[tt_tr] if q == 0 else ())
            Vc(lambda e: e.tensor_scalar(matm[:, :], matm[:, :], 1e-5, None, OP.max))
            Vc(lambda e: e.reciprocal(shinvtm[:, 0:4], matm[:, :]))
            Vc(lambda e: e.tensor_scalar(strow[:, :], shinvtm[:, 0:4], 127.0, None,
                                         OP.mult))
            Vc(lambda e: e.reciprocal(shinvtm[:, 0:4], strow[:, :]))
            Vc(lambda e: e.tensor_scalar(
                scale2tm[:, :], shinvtm[:, 0:4], meanbc2[:, 0:1], None, OP.mult))
            t_sth = Sc(lambda e: e.copy(sthi[:, :], strow[:, :]), waits=[P.tok("vv")])
            tv_st = Vc(lambda e: e.scalar_tensor_tensor(
                stlo[:, :], sthi[:, :], -1.0, strow[:, :], OP.mult, OP.add),
                waits=[t_sth])
            Tn(lambda e: e.matmul(ps[0:4, 3584:3712], sthi[:, :], identB[:, :],
                                  start=True, stop=False), waits=[tv_st, tt_tr])
            Tn(lambda e: e.matmul(ps[0:4, 3584:3712], stlo[:, :], identB[:, :],
                                  start=False, stop=True), inc="tv")
            tt_t2 = P.tok("tv")
            t_rh = Sc(lambda e: e.copy(rstage_h[0:4, 0:128], ps[0:4, 3584:3712]),
                      waits=[tt_t2])
            t_rl = Vc(lambda e: e.scalar_tensor_tensor(
                rstage_l[0:4, 0:128], rstage_h[0:4, 0:128], -1.0, ps[0:4, 3584:3712],
                OP.mult, OP.add), waits=[t_rh])
            for q in range(4):
                Tn(lambda e, q=q: e.matmul(
                    ps[:, 3072 + q * 128: 3072 + (q + 1) * 128],
                    EQb[:, q * 128:(q + 1) * 128], rstage_h[:, 0:128],
                    start=True, stop=False), waits=[t_rl] if q == 0 else ())
                Tn(lambda e, q=q: e.matmul(
                    ps[:, 3072 + q * 128: 3072 + (q + 1) * 128],
                    EQb[:, q * 128:(q + 1) * 128], rstage_l[:, 0:128],
                    start=False, stop=True))
            P.inc_last("tensor", "tv")
            tt_bc = P.tok("tv")
            tv_sbc = Sc(lambda e: e.copy(sbc[:, :], ps[:, 3072:3584]),
                        waits=[tt_bc])

            def ldh(c):
                hb = hsr[c % 4]
                wlh = [t_spill_h[(j, c)]]
                if (j, c - 4) in t_tt2:
                    wlh.append(t_tt2[(j, c - 4)])
                elif j > 0:
                    wlh.append(t_tt2[(j - 1, NH - 4 + c)])
                else:
                    wlh.append(t_attn_t)
                    wlh.append(t_prologue_g)
                t_ldh[(j, c)] = Sy(lambda e, j=j, c=c, hb=hb: e.dma_start(
                    hb[:, :], d_hs[j, c]), waits=wlh)

            def f2_tt(c):
                whq = [tv_sbc, t_ldh[(j, c)]]
                if (j, c - 3) in t_i322:
                    whq.append(t_i322[(j, c - 3)])
                t_tt2[(j, c)] = Vc(lambda e, c=c: e.tensor_tensor(
                    qtmp[c % 3][:, :], hsr[c % 4][:, :], sbc[:, :],
                    OP.mult), waits=whq)

            def f2_magic(c):
                t_sm2[(j, c)] = Vc(lambda e, c=c: e.tensor_scalar(
                    qtmp[c % 3][:, :], qtmp[c % 3][:, :], MAGIC, None, OP.add),
                    waits=[t_tt2[(j, c)]])

            def f2_i32(c):
                w = [t_sm2[(j, c)]]
                if (j, c - 3) in t_hq:
                    w.append(t_hq[(j, c - 3)])
                t_i322[(j, c)] = Vc(lambda e, c=c: e.tensor_scalar(
                    qi32[c % 3][:, :], qtmp[c % 3][:, :], -MAGIC, CFIX,
                    OP.add, OP.add), waits=w)

            def f2_cast(c):
                w = [t_i322[(j, c)]]
                if (j, c - 3) in t_f2mm:
                    w.append(t_f2mm[(j, c - 3)])
                t_hq[(j, c)] = Sc(lambda e, c=c: e.activation(
                    hqb[c % 3][:, :], qi32[c % 3][:, :], AF.Identity,
                    bias=noffbc[:, 0:1]), waits=w)

            def f2_mm(c):
                stw = w2stage[c % 4]
                if c >= 4:
                    wst = [t_f2mm[(j, c - 4)]]
                elif j > 0:
                    wst = [t_f2mm[(j - 1, NH - 4 + c)]]
                else:
                    wst = [t_attn_t, t_prologue_g]
                t_w2st = Sy(lambda e, c=c, stw=stw: e.dma_start(
                    stw[:, :], d_w2q[c]), waits=wst)
                wmm2 = [t_hq[(j, c)], t_w2st]
                if c == 0:
                    wmm2.append(tv_sbc)
                    wmm2.append(P.tok("vv"))
                    wmm2.append(P.tok("sv"))
                for ce in range(ND):
                    Tn(lambda e, c=c, ce=ce: e.matmul(
                        ps[:, ce * 512:(ce + 1) * 512],
                        w2stage[c % 4][:, ce * 128:(ce + 1) * 128], hqb[c % 3][:, :],
                        start=(c == 0), stop=(c == NH - 1)),
                        waits=wmm2 if ce == 0 else ())
                t_f2mm[(j, c)] = P.inc_last("tensor", "tv")

            for cp in range(4):
                ldh(cp)
            for c in range(NH + 1):
                if c < NH:
                    f2_tt(c)
                    f2_magic(c)
                    if c + 4 < NH:
                        ldh(c + 4)
                if c >= 1:
                    f2_i32(c - 1)
                    f2_cast(c - 1)
                    f2_mm(c - 1)
            wy2 = [t_f2mm[(j, NH - 1)]]
            for ce in range(ND):
                t_hi = Sc(lambda e, ce=ce: e.copy(
                    hi2[:, ce * 512:(ce + 1) * 512], ps[:, ce * 512:(ce + 1) * 512]),
                    waits=wy2 if ce == 0 else ())
                Vc(lambda e, ce=ce: e.scalar_tensor_tensor(
                    lo2[:, ce * 512:(ce + 1) * 512], hi2[:, ce * 512:(ce + 1) * 512],
                    -1.0, ps[:, ce * 512:(ce + 1) * 512], OP.mult, OP.add),
                    waits=[t_hi])
            t_y2ev = [P.tok("sv"), P.tok("vv")]

            for q in range(4):
                i = j * 4 + q
                par = q % 2
                pc = 3072
                xr = x2rd[par]
                wldx = [t_spill[i]]
                if i >= 2:
                    wldx.append(t_y2T_ev[i - 2])
                else:
                    wldx.append(t_attn_t)
                tldx2 = Sy(lambda e, i=i, xr=xr: e.dma_start(
                    xr[:, :], d_x2[i * 128:(i + 1) * 128, :]), waits=wldx)
                wtt = [t_y2ev]
                if q >= 1:
                    wtt.append(t_y2T_ev[i - 1])
                for ce in range(ND):
                    Tn(lambda e, ce=ce, q=q, pc=pc: e.matmul(
                        ps[:, pc + ce * 128: pc + (ce + 1) * 128],
                        hi2[:, ce * 512 + q * 128: ce * 512 + (q + 1) * 128],
                        identB[:, :], start=True, stop=False),
                        waits=wtt if ce == 0 else ())
                    Tn(lambda e, ce=ce, q=q, pc=pc: e.matmul(
                        ps[:, pc + ce * 128: pc + (ce + 1) * 128],
                        lo2[:, ce * 512 + q * 128: ce * 512 + (q + 1) * 128],
                        identB[:, :], start=False, stop=True))
                tt_y = P.inc_last("tensor", "tv")
                ob = outb[par]
                wev2 = [tt_y, tldx2]
                if i >= 2:
                    wev2.append(t_ln2[i - 2])
                    wev2.append(t_outs[i - 2])
                t_y2T_ev[i] = Vc(lambda e, pc=pc, q=q, xr=xr, ob=ob: e.scalar_tensor_tensor(
                    ob[:, :], ps[:, pc:pc + 1024], scale2tm[:, q:q + 1], xr[:, :],
                    OP.mult, OP.add), waits=wev2, inc="vv")
                pr = q % 2
                Vc(lambda e, ob=ob, pr=pr: e.bn_stats(st12[pr][:, 0:6], ob[:, 0:512]))
                Vc(lambda e, ob=ob, pr=pr: e.bn_stats(st12[pr][:, 6:12], ob[:, 512:1024]))
                Vc(lambda e, pr=pr: e.bn_aggr(mv[pr][:, :], st12[pr][:, :]))
                tv_v2 = Vc(lambda e, pr=pr: e.tensor_scalar(
                    varp[pr][:, :], mv[pr][:, 1:2], EPS, None, OP.add))
                Sc(lambda e, pr=pr: e.activation(lnv[pr][:, :], varp[pr][:, :], AF.Ln),
                   waits=[tv_v2])
                t_rs2 = Sc(lambda e, pr=pr: e.activation(
                    rs[pr][:, :], lnv[pr][:, :], AF.Exp, scale=-0.5))
                tv_nm2 = Vc(lambda e, pr=pr: e.scalar_tensor_tensor(
                    negmrs[pr][:, :], mv[pr][:, 0:1], -1.0, rs[pr][:, :],
                    OP.mult, OP.mult), waits=[t_rs2])
                wout = [tv_nm2]
                if i >= 2:
                    wout.append(t_outs[i - 2])
                t_ln2[i] = Sc(lambda e, ob=ob, pr=pr: e.activation(
                    ob[:, :], ob[:, :], AF.Identity, bias=negmrs[pr][:, 0:1],
                    scale=rs[pr][:, 0:1]), waits=wout)
                t_outs[i] = Gp(lambda e, i=i, ob=ob, row0=row0: e.dma_start(
                    d_y[row0 + i * 128: row0 + (i + 1) * 128, :], ob[:, :]),
                    waits=[t_ln2[i]])

    pro_drain()

    P.op("gpsimd", lambda e: e.nop(), waits=Qtok("gd"))
    P.op("sync", lambda e: e.nop(), waits=Qtok("ld"))

    with es:
        with nc.Block() as block:
            P.emit(block, sems)
    return nc


_program_cache = {}


def _get_program(T, nseq):
    key = (T, nseq)
    if key not in _program_cache:
        _program_cache[key] = build_program(T, nseq)
    return _program_cache[key]


def _consts():
    identb = np.eye(128, dtype=ml_dtypes.bfloat16)
    e0b = np.zeros((128, 128), dtype=ml_dtypes.bfloat16)
    e0b[0, :] = 1
    eqb = np.zeros((128, 512), dtype=ml_dtypes.bfloat16)
    for q in range(4):
        eqb[q, q * 128:(q + 1) * 128] = 1
    onesb = np.ones((128, 1), dtype=ml_dtypes.bfloat16)
    maskf = np.where(np.arange(128)[None, :] > np.arange(128)[:, None],
                     np.float32(NEG_BIG), np.float32(0.0)).astype(np.float32)
    return identb, e0b, eqb, onesb, maskf


def kernel(L0_attn_x, L0_attn_Wq, L0_attn_Wk, L0_attn_Wv,
           L0_ffn_w1, L0_ffn_conv_w, L0_ffn_conv_b, L0_ffn_w2):
    x = np.ascontiguousarray(np.asarray(L0_attn_x, dtype=np.float32))
    wq = np.ascontiguousarray(np.asarray(L0_attn_Wq, dtype=np.float32).T)
    wk = np.ascontiguousarray(np.asarray(L0_attn_Wk, dtype=np.float32).T)
    wv = np.ascontiguousarray(np.asarray(L0_attn_Wv, dtype=np.float32))
    w1t = np.ascontiguousarray(np.asarray(L0_ffn_w1, dtype=np.float32).T)
    w2t = np.ascontiguousarray(np.asarray(L0_ffn_w2, dtype=np.float32).T)
    cw = np.ascontiguousarray(
        np.asarray(L0_ffn_conv_w, dtype=np.float32).reshape(NH, 128, 3)
        .transpose(1, 0, 2).reshape(128, NH * 3))
    cb = np.ascontiguousarray(
        np.asarray(L0_ffn_conv_b, dtype=np.float32).reshape(NH, 128).T)
    identb, e0b, eqb, onesb, maskf = _consts()

    B, T, _ = x.shape
    nseq = B // N_CORES
    nc = _get_program(T, nseq)
    in_maps = []
    for c in range(N_CORES):
        in_maps.append({
            "x": np.ascontiguousarray(x[c * nseq:(c + 1) * nseq].reshape(nseq * T, D)),
            "wq": wq, "wk": wk, "wv": wv, "w1t": w1t, "w2t": w2t,
            "cw": cw, "cb": cb, "identb": identb, "e0b": e0b, "eqb": eqb,
            "onesb": onesb, "maskf": maskf,
        })
    res = run_bass_kernel_spmd(nc, in_maps, core_ids=list(range(N_CORES)))
    out = np.concatenate(
        [r["y"].reshape(nseq, T, D) for r in res.results], axis=0)
    return out.astype(np.float32)


# version 28 (final)
# speedup vs baseline: 1.0126x; 1.0126x over previous
import numpy as np
import ml_dtypes
from contextlib import ExitStack

import concourse.bass as bass
import concourse.mybir as mybir
from concourse.bass_utils import run_bass_kernel_spmd

F32 = mybir.dt.float32
BF16 = mybir.dt.bfloat16
I32 = mybir.dt.int32
AF = mybir.ActivationFunctionType
OP = mybir.AluOpType
AX = mybir.AxisListType

MAGIC = 12582912.0
OFF = 256.0
CFIX = OFF + 0.5 - 2.0 ** -15
CLIP15 = 1.5 - 2.0 ** -22
NEG_BIG = -1.0e30
EPS = 1e-5

D = 1024
HID = 4096
ND = D // 128
NH = HID // 128
N_CORES = 8


class Prog:
    ENGS = ("sync", "gpsimd", "vector", "scalar", "tensor")
    AUTO = {"vector": "vv", "scalar": "sv"}

    def __init__(self):
        self.streams = {e: [] for e in self.ENGS}
        self.cnt = {}

    @staticmethod
    def _flat(waits):
        out = []
        for t in waits:
            if t is None:
                continue
            if isinstance(t, list):
                out.extend(x for x in t if x is not None)
            else:
                out.append(t)
        return out

    def op(self, eng, fn, waits=(), inc=None, inc_n=1):
        w = [(s, v) for (s, v) in self._flat(waits) if v > 0]
        if eng in self.AUTO:
            sem = self.AUTO[eng]
            assert inc is None or inc == sem
            idx = self.cnt.get(sem, 0) + 1
            self.cnt[sem] = idx
            chain = (sem, idx - 1)
            self.streams[eng].append((fn, w, sem, 1, chain))
            return (sem, idx)
        self.streams[eng].append((fn, w, inc, inc_n, None))
        if inc is not None:
            self.cnt[inc] = self.cnt.get(inc, 0) + inc_n
            return (inc, self.cnt[inc])
        return None

    def tok(self, sem):
        return (sem, self.cnt.get(sem, 0))

    def inc_last(self, eng, sem, n=1):
        fn, w, inc, inc_n, chain = self.streams[eng][-1]
        assert inc is None, "last op already has an inc"
        self.streams[eng][-1] = (fn, w, sem, n, chain)
        self.cnt[sem] = self.cnt.get(sem, 0) + n
        return (sem, self.cnt[sem])

    def emit(self, block, sems):
        for eng in self.ENGS:
            recs = self.streams[eng]

            def body(e, recs=recs, eng=eng):
                seen = {}
                for fn, waits, inc, inc_n, chain in recs:
                    for s, v in waits:
                        if v > seen.get(s, 0):
                            e.wait_ge(sems[s], v)
                            seen[s] = v
                    ins = fn(e)
                    if chain is not None and chain[1] > seen.get(chain[0], 0):
                        ins._wait_ge(sems[chain[0]], chain[1])
                        seen[chain[0]] = chain[1]
                    if inc is not None:
                        ins.then_inc(sems[inc], inc_n)
            getattr(block, eng)(body)


def build_program(T=2048, nseq=4, phases=5):
    NT = T // 128
    NBL = T // 512
    XQW = T + 2

    nc = bass.Bass("TRN2", target_bir_lowering=False)

    d_x = nc.dram_tensor("x", [nseq * T, D], F32, kind="ExternalInput").ap()
    d_y = nc.dram_tensor("y", [nseq * T, D], F32, kind="ExternalOutput").ap()
    d_wq = nc.dram_tensor("wq", [D, D], F32, kind="ExternalInput").ap()
    d_wk = nc.dram_tensor("wk", [D, D], F32, kind="ExternalInput").ap()
    d_wv = nc.dram_tensor("wv", [D, D], F32, kind="ExternalInput").ap()
    d_w1t = nc.dram_tensor("w1t", [D, HID], F32, kind="ExternalInput").ap()
    d_w2t = nc.dram_tensor("w2t", [HID, D], F32, kind="ExternalInput").ap()
    d_cw = nc.dram_tensor("cw", [128, NH * 3], F32, kind="ExternalInput").ap()
    d_cb = nc.dram_tensor("cb", [128, NH], F32, kind="ExternalInput").ap()
    d_id = nc.dram_tensor("identb", [128, 128], BF16, kind="ExternalInput").ap()
    d_e0 = nc.dram_tensor("e0b", [128, 128], BF16, kind="ExternalInput").ap()
    d_eq = nc.dram_tensor("eqb", [128, 512], BF16, kind="ExternalInput").ap()
    d_ones = nc.dram_tensor("onesb", [128, 1], BF16, kind="ExternalInput").ap()
    d_mask = nc.dram_tensor("maskf", [128, 128], F32, kind="ExternalInput").ap()
    d_wkb = nc.dram_tensor("wkb_s", [128, ND * D], BF16, kind="Internal").ap()
    d_wvb = nc.dram_tensor("wvb_s", [128, ND * D], BF16, kind="Internal").ap()
    d_w1q = nc.dram_tensor("w1q_s", [NH, 128, ND * 128], BF16, kind="Internal").ap()
    d_w2q = nc.dram_tensor("w2q_s", [NH, 128, ND * 128], BF16, kind="Internal").ap()
    d_x2 = nc.dram_tensor("x2_s", [T, D], F32, kind="Internal").ap()
    d_hs = nc.dram_tensor("hs_s", [NBL, NH, 128, 512], F32, kind="Internal").ap()
    d_sxh = nc.dram_tensor("sxh_s", [T], BF16, kind="Internal").ap()
    d_sxl = nc.dram_tensor("sxl_s", [T], BF16, kind="Internal").ap()
    d_shh = nc.dram_tensor("shh_s", [512], BF16, kind="Internal").ap()
    d_shl = nc.dram_tensor("shl_s", [512], BF16, kind="Internal").ap()

    es = ExitStack()

    def SB(name, shape, dt):
        return es.enter_context(nc.sbuf_tensor(name, shape, dt))

    identB = SB("identB", [128, 128], BF16)
    E0b = SB("E0b", [128, 128], BF16)
    EQb = SB("EQb", [128, 512], BF16)
    onesb = SB("onesb_sb", [128, 1], BF16)
    maskf = SB("maskf_sb", [128, 128], F32)
    convw = SB("convw", [128, NH * 3], F32)
    convb = SB("convb", [128, NH], F32)
    meanbc1 = SB("meanbc1", [128, 1], F32)
    meanbc2 = SB("meanbc2", [128, 1], F32)
    sq1bc = SB("sq1bc", [128, 1], F32)
    sq2bc = SB("sq2bc", [128, 1], F32)
    sxinvtm = SB("sxinvtm", [128, NT], F32)
    sxhi = SB("sxhi", [128, NT], BF16)
    sxlo = SB("sxlo", [128, NT], BF16)
    mrow = SB("mrow", [128, 1], F32)
    negm32 = SB("negm32", [128, 1], F32)
    lsum = SB("lsum", [128, 1], F32)
    linv = SB("linv", [128, 1], F32)
    st12 = [SB("st12a", [128, 12], F32), SB("st12b", [128, 12], F32)]
    mv = [SB("mva", [128, 2], F32), SB("mvb", [128, 2], F32)]
    varp = [SB("varpa", [128, 1], F32), SB("varpb", [128, 1], F32)]
    lnv = [SB("lnva", [128, 1], F32), SB("lnvb", [128, 1], F32)]
    rs = [SB("rsa", [128, 1], F32), SB("rsb", [128, 1], F32)]
    negmrs = [SB("negmrsa", [128, 1], F32), SB("negmrsb", [128, 1], F32)]
    amax = SB("amax", [128, 1], F32)
    arec = SB("arec", [128, 1], F32)
    scol = SB("scol", [128, 1], F32)
    matm = SB("matm", [128, 4], F32)
    strow = SB("strow", [128, 4], F32)
    sthi = SB("sthi", [128, 4], BF16)
    stlo = SB("stlo", [128, 4], BF16)
    shinvtm = SB("shinvtm", [128, 4], F32)
    scale2tm = SB("scale2tm", [128, 4], F32)
    psums1 = SB("psums1", [128, 16], F32)
    psums2 = SB("psums2", [128, 16], F32)
    redtmp = SB("redtmp", [128, 1], F32)
    redhi = SB("redhi", [128, 1], BF16)
    redlo = SB("redlo", [128, 1], BF16)
    sc11 = SB("sc11", [1, 1], F32)
    schi = SB("schi", [128, 1], BF16)
    sclo = SB("sclo", [128, 1], BF16)
    magicbc = SB("magicbc", [128, 1], F32)
    noffbc = SB("noffbc", [128, 1], F32)
    rstage_h = SB("rstage_h", [128, 512], BF16)
    rstage_l = SB("rstage_l", [128, 512], BF16)
    XQ2T = SB("XQ2T", [128, ND * XQW], BF16)
    sxinvb = SB("sxinvb", [128, T + 2], F32)
    WqB = SB("WqB", [128, ND * D], BF16)

    base = (nc.sbuf_base + 63) & ~63

    def AT(name, shape, dt, off):
        return nc.alloc_sbuf_tensor_at(name, shape, dt, offset=base + off)

    XTq = AT("XTq", [128, ND * T], BF16, 0)
    oL = ND * T * 2
    w1stage = [AT(f"w1s{k}", [128, ND * 128], BF16, oL + 2048 * k) for k in range(3)]
    hsf = [AT("hsf0", [128, 514], F32, oL + 6144),
           AT("hsf1", [128, 514], F32, oL + 8256)]
    ctmp = [AT("ctmp0", [128, 512], F32, oL + 10368),
            AT("ctmp1", [128, 512], F32, oL + 12416)]
    maxbufs = [AT(f"maxb{k}", [128, 512], F32, oL + 14464 + 2048 * k)
               for k in range(NBL)]
    negbufs = [AT(f"negb{k}", [128, 512], F32, oL + 14464 + 2048 * NBL + 2048 * k)
               for k in range(NBL)]
    assert 14464 + 4096 * NBL <= 32768
    Vtm = AT("Vtm", [128, NT * D], BF16, oL + 32768)
    o4 = oL + 32768 + NT * D * 2
    QT = AT("QT", [128, ND * 512], BF16, o4)
    Wv2 = AT("Wv2", [128, ND * D], BF16, o4)
    Pbuf = AT("Pbuf", [128, T], BF16, o4 + 8192)
    PT = AT("PT", [128, T], BF16, o4 + 8192 + T * 2)
    o5 = o4 + max(16384, 8192 + 2 * T * 2)
    xstage = [AT("xstage0", [128, D], F32, o5),
              AT("xstage1", [128, D], F32, o5 + 4096)]
    o6 = o5 + 8192
    t1buf = AT("t1buf", [128, D], F32, o6)
    x2buf = [AT("x2buf0", [128, D], F32, o6 + 4096),
             AT("x2buf1", [128, D], F32, o6 + 8192)]
    ytmp = AT("ytmp", [128, D], F32, o6 + 12288)
    i32t = AT("i32t", [128, D], I32, o6 + 16384)
    xq2 = AT("xq2", [128, D], BF16, o6 + 20480)
    WkB = AT("WkB", [128, ND * D], BF16, o6)
    xbstage = [AT("xbstage0", [128, D], BF16, o6 + 16384),
               AT("xbstage1", [128, D], BF16, o6 + 18432)]
    region_end_a = o6 + 22528

    oF = o4
    hsr = [AT(f"hsr{k}", [128, 512], F32, oF + 2048 * k) for k in range(3)]
    qtmp = [AT("qtmp0", [128, 512], F32, oF + 6144),
            AT("qtmp1", [128, 512], F32, oF + 8192)]
    qi32 = [AT("qi32_0", [128, 512], I32, oF + 10240),
            AT("qi32_1", [128, 512], I32, oF + 12288)]
    hqb = [AT("hq0", [128, 512], BF16, oF + 14336),
           AT("hq1", [128, 512], BF16, oF + 15360),
           AT("hq2", [128, 512], BF16, oF + 30720)]
    w2stage = [AT(f"w2s{k}", [128, ND * 128], BF16, oF + 16384 + 2048 * k)
               for k in range(4)]
    mxf = AT("mxf", [128, 512], F32, oF + 24576)
    mxhi = AT("mxhi", [128, 512], BF16, oF + 26624)
    mxlo = AT("mxlo", [128, 512], BF16, oF + 27648)
    sbc = AT("sbc", [128, 512], F32, oF + 28672)
    assert oF + 31744 <= region_end_a
    oV = oL + 32768 if NT * D * 2 >= 32768 else region_end_a
    hi2 = AT("hi2", [128, ND * 512], BF16, oV)
    lo2 = AT("lo2", [128, ND * 512], BF16, oV + 8192)
    x2rd = [AT("x2rd0", [128, D], F32, oV + 16384),
            AT("x2rd1", [128, D], F32, oV + 20480)]
    outb = [AT("outb0", [128, D], F32, oV + 24576),
            AT("outb1", [128, D], F32, oV + 28672)]
    region_end_a = max(region_end_a, oV + 32768)
    pstgh = [AT("pstgh0", [128, 2048], F32, oL),
             AT("pstgh1", [128, 2048], F32, oL + 8192)]
    pi32h = [AT("pi32h0", [128, 2048], I32, oL),
             AT("pi32h1", [128, 2048], I32, oL + 8192)]
    pternh = [AT("pternh0", [128, 2048], BF16, oL + 16384),
              AT("pternh1", [128, 2048], BF16, oL + 20480)]
    assert base + region_end_a <= nc.sbuf_top, (
        f"SBUF overflow: {base}+{region_end_a} > {nc.sbuf_top}")

    ps = es.enter_context(nc.psum_tensor("ps", [128, 4096], F32))

    NLANE = 4
    sem_names = (["tv", "vv", "sv"]
                 + [f"ld{i}" for i in range(NLANE)]
                 + [f"gd{i}" for i in range(NLANE)])
    sems = {n: es.enter_context(nc.semaphore(n)) for n in sem_names}
    P = Prog()
    _rr = {"ld": 0, "gd": 0}

    def _dma(eng, queue, fn, waits):
        lane = f"{queue}{_rr[queue] % NLANE}"
        _rr[queue] += 1
        w = list(waits) + [P.tok(lane)]
        return P.op(eng, fn, w, lane, 16)

    def Sy(fn, waits=()):
        return _dma("sync", "ld", fn, waits)

    def Gp(fn, waits=()):
        return _dma("gpsimd", "gd", fn, waits)

    def Qtok(queue):
        return [P.tok(f"{queue}{i}") for i in range(NLANE)]

    def Tn(fn, waits=(), inc=None):
        return P.op("tensor", fn, waits, inc, 1)

    def Vc(fn, waits=(), inc=None):
        return P.op("vector", fn, waits, inc, 1)

    def Sc(fn, waits=(), inc=None):
        return P.op("scalar", fn, waits, inc, 1)

    def XQ3(cd_count=ND):
        return XQ2T[:, :].rearrange("p (c w) -> p c w", w=XQW)

    Sy(lambda e: e.dma_start(identB[:, :], d_id))
    Sy(lambda e: e.dma_start(E0b[:, :], d_e0))
    Sy(lambda e: e.dma_start(EQb[:, :], d_eq))
    Sy(lambda e: e.dma_start(onesb[:, :], d_ones))
    Sy(lambda e: e.dma_start(maskf[:, :], d_mask))
    Sy(lambda e: e.dma_start(convw[:, :], d_cw))
    Sy(lambda e: e.dma_start(convb[:, :], d_cb))
    t_const = Qtok("ld")
    Vc(lambda e: e.memset(XQ3()[:, :, 0], 0.0), waits=[])
    Vc(lambda e: e.memset(XQ3()[:, :, XQW - 1], 0.0))
    Vc(lambda e: e.memset(sxinvb[:, 0:1], 0.0))
    Vc(lambda e: e.memset(sxinvb[:, T + 1:T + 2], 0.0))
    Vc(lambda e: e.memset(rstage_h[:, :], 0.0))
    Vc(lambda e: e.memset(rstage_l[:, :], 0.0))
    Vc(lambda e: e.memset(magicbc[:, :], MAGIC))
    Vc(lambda e: e.memset(noffbc[:, :], -OFF))
    Vc(lambda e: e.memset(schi[:, :], 0.0))
    Vc(lambda e: e.memset(sclo[:, :], 0.0), inc="vv")
    t_guards = P.tok("vv")

    def _w3(dap):
        return dap.rearrange("(c p) e -> p c e", p=128)

    def _s3(sap):
        return sap.rearrange("p (c e) -> p c e", e=D)

    tg_wq = Gp(lambda e: e.dma_start(_s3(WkB[:, :]), _w3(d_wq)))
    tg_wk = Gp(lambda e: e.dma_start(_s3(Wv2[:, :]), _w3(d_wk)))
    t_mev = {}
    for i_m in range(ND):
        for jh in range(2):
            wmb = [tg_wq, tg_wk]
            if (i_m - 1, jh) in t_mev:
                wmb.append(t_mev[(i_m - 1, jh)])
            for ce in range(ND):
                Tn(lambda e, i_m=i_m, jh=jh, ce=ce: e.matmul(
                    ps[:, 2048 + jh * 512: 2048 + (jh + 1) * 512],
                    WkB[:, ce * D + i_m * 128: ce * D + i_m * 128 + 128],
                    Wv2[:, ce * D + jh * 512: ce * D + (jh + 1) * 512],
                    start=(ce == 0), stop=(ce == ND - 1)),
                    waits=wmb if ce == 0 else ())
            tmb = P.inc_last("tensor", "tv")
            t_mev[(i_m, jh)] = Vc(lambda e, i_m=i_m, jh=jh: e.tensor_copy(
                WqB[:, i_m * D + jh * 512: i_m * D + (jh + 1) * 512],
                ps[:, 2048 + jh * 512: 2048 + (jh + 1) * 512]), waits=[tmb])
    P.op("vector", lambda e: e.nop(), inc="vv")
    t_mdone = P.tok("vv")
    tg_wv = Gp(lambda e: e.dma_start(_s3(WkB[:, :]), _w3(d_wv)),
               waits=[P.tok("tv")])
    t_wvst = Gp(lambda e: e.dma_start(d_wvb, WkB[:, :]), waits=[tg_wv])

    def _ld1(e, hk, st):
        cd, k = hk // 2, hk % 2
        return e.dma_start(st[:, :],
                           d_w1t[cd * 128:(cd + 1) * 128, k * 2048:(k + 1) * 2048])

    def _ld2(e, fk, st):
        r0 = fk * 256
        return e.dma_start(st[:, :].rearrange("p (g e) -> p g e", e=D),
                           d_w2t[r0:r0 + 256, :].rearrange("(g p) e -> p g e", p=128))

    pro_steps = []
    tql = {}

    def _p1w1(hk):
        st = pstgh[hk % 2]
        w = [tql[hk - 2]] if hk >= 2 else []
        tld = Sy(lambda e, hk=hk, st=st: _ld1(e, hk, st), waits=w)
        tql[hk] = Sc(lambda e, hk=hk, st=st: e.activation(
            st[:, :], st[:, :], AF.Abs, accum_out=psums1[:, hk:hk + 1]),
            waits=[tld])

    def _p1w2(fk):
        st = pstgh[fk % 2]
        tld = Gp(lambda e, fk=fk, st=st: _ld2(e, fk, st),
                 waits=[tql[16 + fk - 2]] if fk >= 2 else [tql[15]])
        tql[16 + fk] = Sc(lambda e, fk=fk, st=st: e.activation(
            st[:, :], st[:, :], AF.Abs, accum_out=psums2[:, fk:fk + 1]),
            waits=[tld])

    for _hk in range(16):
        pro_steps.append(lambda hk=_hk: _p1w1(hk))
    for _fk in range(16):
        pro_steps.append(lambda fk=_fk: _p1w2(fk))

    def mean_pipeline(psumsbuf, n, scale, meanbc, sqbc, w0=()):
        Vc(lambda e: e.tensor_reduce(redtmp[:, :], psumsbuf[:, 0:n], AX.X, OP.add, False),
           waits=w0)
        Vc(lambda e: e.tensor_copy(redhi[:, :], redtmp[:, :]))
        Vc(lambda e: e.scalar_tensor_tensor(redlo[:, :], redhi[:, :], -1.0, redtmp[:, :],
                                            OP.mult, OP.add), inc="vv")
        tv1 = P.tok("vv")
        Tn(lambda e: e.matmul(ps[0:1, 3072:3073], onesb[:, :], redhi[:, :],
                              start=True, stop=False), waits=[tv1, t_const])
        Tn(lambda e: e.matmul(ps[0:1, 3072:3073], onesb[:, :], redlo[:, :],
                              start=False, stop=True), inc="tv")
        tt1 = P.tok("tv")
        Vc(lambda e: e.tensor_scalar(sc11[:, :], ps[0:1, 3072:3073], scale, 1e-5,
                                     OP.mult, OP.max), waits=[tt1])
        Vc(lambda e: e.tensor_copy(schi[0:1, :], sc11[:, :]), waits=[t_guards])
        Vc(lambda e: e.scalar_tensor_tensor(sclo[0:1, :], schi[0:1, :], -1.0, sc11[:, :],
                                            OP.mult, OP.add), inc="vv")
        tv2 = P.tok("vv")
        Tn(lambda e: e.matmul(ps[:, 3584:3585], E0b[:, :], schi[:, :],
                              start=True, stop=False), waits=[tv2])
        Tn(lambda e: e.matmul(ps[:, 3584:3585], E0b[:, :], sclo[:, :],
                              start=False, stop=True), inc="tv")
        tt2 = P.tok("tv")
        Vc(lambda e: e.tensor_copy(meanbc[:, :], ps[:, 3584:3585]), waits=[tt2])
        Vc(lambda e: e.reciprocal(sqbc[:, :], meanbc[:, :]), inc="vv")

    tok_box = {}

    def _means_all():
        mean_pipeline(psums1, 16, 1.0 / (D * HID), meanbc1, sq1bc, w0=[tql[31]])
        mean_pipeline(psums2, 16, 1.0 / (D * HID), meanbc2, sq2bc, w0=[tql[31]])
        tok_box["means"] = P.tok("vv")

    pro_steps.append(_means_all)
    MEANS_STEP = len(pro_steps)

    def ternarize_h(par, sqbc, tld, tst_prev):
        st, pi, pt = pstgh[par], pi32h[par], pternh[par]
        Vc(lambda e, st=st: e.tensor_scalar(st[:, :], st[:, :], sqbc[:, 0:1], CLIP15,
                                            OP.mult, OP.min), waits=[tld])
        Vc(lambda e, st=st: e.tensor_scalar(st[:, :], st[:, :], -CLIP15, MAGIC,
                                            OP.max, OP.add))
        tv3 = Vc(lambda e, st=st, pi=pi: e.tensor_scalar(pi[:, :], st[:, :], -MAGIC, CFIX,
                                                         OP.add, OP.add))
        tc = Sc(lambda e, pi=pi, pt=pt: e.activation(
            pt[:, :], pi[:, :], AF.Identity, bias=noffbc[:, 0:1]),
            waits=[tv3, tst_prev])
        return tv3, tc

    tqs = {}

    def _p2w2(fk):
        par = fk % 2
        w = [tok_box["means"]]
        if fk >= 2:
            w.extend(tqs[fk - 2][:2])
        tld = Gp(lambda e, fk=fk, par=par: _ld2(e, fk, pstgh[par]), waits=w)
        tv3, tc = ternarize_h(par, sq2bc, tld,
                              tqs[fk - 2][2] if fk >= 2 else None)
        tst = Gp(lambda e, fk=fk, par=par: e.dma_start(
            d_w2q[2 * fk:2 * fk + 2].transpose([1, 0, 2]),
            pternh[par][:, :].rearrange("p (g e) -> p g e", e=D)), waits=[tc])
        tqs[fk] = (tv3, tc, tst)

    def _p2w1(hk):
        par = hk % 2
        gk = 16 + hk
        w = list(tqs[gk - 2][:2])
        tld = Sy(lambda e, hk=hk, par=par: _ld1(e, hk, pstgh[par]), waits=w)
        tv3, tc = ternarize_h(par, sq1bc, tld, tqs[gk - 2][2])
        cd, k = hk // 2, hk % 2
        tst = Gp(lambda e, cd=cd, k=k, par=par: e.dma_start(
            d_w1q[k * 16:(k + 1) * 16, :, cd * 128:(cd + 1) * 128].transpose([1, 0, 2]),
            pternh[par][:, :].rearrange("p (c h) -> p c h", h=128)), waits=[tc])
        tqs[gk] = (tv3, tc, tst)

    for _fk in range(16):
        pro_steps.append(lambda fk=_fk: _p2w2(fk))
    for _hk in range(16):
        pro_steps.append(lambda hk=_hk: _p2w1(hk))

    pro_idx = [0]

    def pro_pump(n):
        k = 0
        while pro_idx[0] < len(pro_steps) and k < n:
            pro_steps[pro_idx[0]]()
            pro_idx[0] += 1
            k += 1

    def pro_drain(upto=None):
        tgt = len(pro_steps) if upto is None else upto
        while pro_idx[0] < tgt:
            pro_steps[pro_idx[0]]()
            pro_idx[0] += 1

    t_prologue_g = None

    PUMP_Q, PUMP_S, PUMP_TILE, PUMP_BC = 1, 2, 8, 2
    ffn1_ready = []
    unlock_tok = {}
    gck = [0]
    g_hmm, g_mainev, g_conv, g_tree, g_spill = {}, {}, {}, {}, {}
    g_evLR = {}
    g_sil = {}
    t_spill_h = {}
    t_treeblk = {}

    def emit_chunk(jj, c, dummies):
        gc = gck[0]
        gck[0] += 1
        t0 = jj * 512
        stw = w1stage[gc % 3]
        hsb = hsf[gc % 2]
        ct = ctmp[gc % 2]
        wld = [g_hmm[gc - 3]] if gc >= 3 else [t_prologue_g]
        tld = Sy(lambda e, c=c, stw=stw: e.dma_start(stw[:, :], d_w1q[c]), waits=wld)
        ul = unlock_tok[jj]
        whL = [tld, ul, g_evLR.get(gc - 1)]
        for cd in range(ND):
            Tn(lambda e, cd=cd, stw=stw, t0=t0: e.matmul(
                ps[:, 3072:3073], stw[:, cd * 128:(cd + 1) * 128],
                XQ2T[:, cd * XQW + t0: cd * XQW + t0 + 1],
                start=(cd == 0), stop=(cd == ND - 1)), waits=whL if cd == 0 else ())
        for cd in range(ND):
            Tn(lambda e, cd=cd, stw=stw, t0=t0: e.matmul(
                ps[:, 3073:3074], stw[:, cd * 128:(cd + 1) * 128],
                XQ2T[:, cd * XQW + t0 + 513: cd * XQW + t0 + 514],
                start=(cd == 0), stop=(cd == ND - 1)))
        t_hLR = P.inc_last("tensor", "tv")
        wmn = [g_mainev.get(gc - 1)] if gc >= 1 else [tld, ul]
        for cd in range(ND):
            Tn(lambda e, cd=cd, stw=stw, t0=t0: e.matmul(
                ps[:, 3584:4096], stw[:, cd * 128:(cd + 1) * 128],
                XQ2T[:, cd * XQW + 1 + t0: cd * XQW + 1 + t0 + 512],
                start=(cd == 0), stop=(cd == ND - 1)),
                waits=wmn if cd == 0 else ())
        g_hmm[gc] = P.inc_last("tensor", "tv")
        Vc(lambda e, hsb=hsb, t0=t0: e.tensor_tensor(
            hsb[:, 0:1], ps[:, 3072:3073], sxinvb[:, t0:t0 + 1], OP.mult),
            waits=[t_hLR, ul, g_conv.get(gc - 2)])
        g_evLR[gc] = Vc(lambda e, hsb=hsb, t0=t0: e.tensor_tensor(
            hsb[:, 513:514], ps[:, 3073:3074], sxinvb[:, t0 + 513:t0 + 514],
            OP.mult))
        if dummies:
            Tn(lambda e: e.matmul(ps[0:64, 0:64], identB[:, 0:64], identB[:, 0:64],
                                  start=True, stop=True))
            Tn(lambda e: e.matmul(ps[0:64, 0:64], identB[:, 0:64], identB[:, 0:64],
                                  start=True, stop=True))
        g_mainev[gc] = Vc(lambda e, hsb=hsb, t0=t0: e.tensor_tensor(
            hsb[:, 1:513], ps[:, 3584:4096], sxinvb[:, 1 + t0:1 + t0 + 512],
            OP.mult), waits=[g_hmm[gc]])
        return (jj, c, gc)

    def chunk_mid(st):
        jj, c, gc = st
        hsb = hsf[gc % 2]
        ct = ctmp[gc % 2]
        t_tap = Sc(lambda e, ct=ct, hsb=hsb, c=c: e.activation(
            ct[:, :], hsb[:, 0:512], AF.Copy, scale=convw[:, 3 * c:3 * c + 1]),
            waits=[g_mainev[gc], g_spill.get(gc - 2), g_tree.get(gc - 2)])
        Vc(lambda e, ct=ct, hsb=hsb, c=c: e.scalar_tensor_tensor(
            ct[:, :], hsb[:, 1:513], convw[:, 3 * c + 1:3 * c + 2], ct[:, :],
            OP.mult, OP.add), waits=[t_tap])
        g_conv[gc] = Vc(lambda e, ct=ct, hsb=hsb, c=c: e.scalar_tensor_tensor(
            ct[:, :], hsb[:, 2:514], convw[:, 3 * c + 2:3 * c + 3], ct[:, :],
            OP.mult, OP.add))
        g_sil[gc] = Sc(lambda e, ct=ct, c=c: e.activation(
            ct[:, :], ct[:, :], AF.Silu, bias=convb[:, c:c + 1]),
            waits=[g_conv[gc]])

    def chunk_back(st):
        jj, c, gc = st
        ct = ctmp[gc % 2]
        t_sil = g_sil[gc]
        g_spill[gc] = t_spill_h[(jj, c)] = Gp(
            lambda e, jj=jj, c=c, ct=ct: e.dma_start(d_hs[jj, c], ct[:, :]),
            waits=[t_sil])
        mb, nb = maxbufs[jj], negbufs[jj]
        if c == 0:
            Vc(lambda e, mb=mb, ct=ct: e.tensor_copy(mb[:, :], ct[:, :]),
               waits=[t_sil, t_prologue_g])
            g_tree[gc] = t_treeblk[jj] = Vc(
                lambda e, nb=nb, ct=ct: e.tensor_scalar(
                    nb[:, :], ct[:, :], -1.0, None, OP.mult))
        else:
            Vc(lambda e, mb=mb, ct=ct: e.tensor_tensor(
                mb[:, :], mb[:, :], ct[:, :], OP.max), waits=[t_sil])
            g_tree[gc] = t_treeblk[jj] = Vc(
                lambda e, nb=nb, ct=ct: e.scalar_tensor_tensor(
                    nb[:, :], ct[:, :], -1.0, nb[:, :], OP.mult, OP.max))

    fstates = []
    stage_idx = [0, 0]

    def pump(n, dummies=False):
        k = 0
        while ffn1_ready and k < n:
            jj, c = ffn1_ready.pop(0)
            fstates.append(emit_chunk(jj, c, dummies))
            if stage_idx[0] < len(fstates) - 1:
                chunk_mid(fstates[stage_idx[0]])
                stage_idx[0] += 1
            if stage_idx[1] < stage_idx[0] - 1:
                chunk_back(fstates[stage_idx[1]])
                stage_idx[1] += 1
            k += 1

    def pump_drain():
        while stage_idx[0] < len(fstates) or stage_idx[1] < len(fstates):
            if stage_idx[0] < len(fstates):
                chunk_mid(fstates[stage_idx[0]])
                stage_idx[0] += 1
            if stage_idx[1] < stage_idx[0]:
                chunk_back(fstates[stage_idx[1]])
                stage_idx[1] += 1

    KJ = 512

    for s in range(nseq if phases >= 2 else 0):
        row0 = s * T
        t_prev_t = P.tok("tv")
        t_prev_v = P.tok("vv")
        t_prev_s = P.tok("sv")
        t_prev_g = Qtok("gd")

        t_cast = {}
        t_evac = {}
        for i in range(NT):
            stx = xstage[i % 2]
            stb = xbstage[i % 2]
            if i >= 2:
                w = [t_cast[i - 2]]
            elif s == 0:
                w = []
            else:
                w = [t_prev_v, t_prev_s, t_prev_g]
            tld = Sy(lambda e, i=i, stx=stx, row0=row0: e.dma_start(
                stx[:, :], d_x[row0 + i * 128: row0 + (i + 1) * 128, :]), waits=w)
            wc = [tld]
            if i >= 2:
                wc.append(t_evac[i - 2])
            t_cast[i] = Vc(lambda e, stx=stx, stb=stb: e.tensor_copy(stb[:, :], stx[:, :]),
                           waits=wc, inc="vv")
            pc = (i % 2) * 1024
            wts = [t_cast[i], t_guards]
            if i >= 2:
                wts.append(t_evac[i - 2])
            for cd in range(ND):
                Tn(lambda e, cd=cd, pc=pc, stb=stb: e.matmul(
                    ps[:, pc + cd * 128: pc + (cd + 1) * 128],
                    stb[:, cd * 128:(cd + 1) * 128], identB[:, :],
                    start=True, stop=True), waits=wts if cd == 0 else ())
            tt = P.inc_last("tensor", "tv")
            t_evac[i] = Vc(lambda e, i=i, pc=pc: e.tensor_copy(
                XTq[:, :].rearrange("p (c t) -> p c t", t=T)[:, :, i * 128:(i + 1) * 128],
                ps[:, pc:pc + 1024].rearrange("p (c t) -> p c t", t=128)),
                waits=[tt], inc="vv")
            pro_pump(1)
        t_xtq = P.tok("vv")

        if phases < 3:
            continue
        tldv = Sy(lambda e: e.dma_start(Wv2[:, :], d_wvb),
                  waits=([t_mdone, t_wvst] if s == 0 else
                         [t_prev_t, t_prev_g, t_mdone]))
        t_kdone = P.tok("sv")
        tve = {}
        for i in range(NT):
            pc = (i % 2) * 1024
            wts = [tldv, t_xtq]
            if i >= 2:
                wts.append(tve[i - 2])
            else:
                wts.append(t_evac[NT - 1])
            for cd in range(ND):
                for h in range(2):
                    Tn(lambda e, i=i, cd=cd, h=h, pc=pc: e.matmul(
                        ps[:, pc + h * 512: pc + (h + 1) * 512],
                        XTq[:, cd * T + i * 128: cd * T + (i + 1) * 128],
                        Wv2[:, cd * D + h * 512: cd * D + (h + 1) * 512],
                        start=(cd == 0), stop=(cd == ND - 1)),
                        waits=wts if (cd == 0 and h == 0) else ())
            tt = P.inc_last("tensor", "tv")
            tve[i] = Sc(lambda e, i=i, pc=pc: e.copy(
                Vtm[:, i * D:(i + 1) * D], ps[:, pc:pc + 1024]), waits=[tt])
            pro_pump(2)
        t_vdone = P.tok("sv")

        if phases < 4:
            continue
        w45 = [t_mdone, t_prev_s, t_prev_v]
        q03 = {k: None for k in range(4)}
        t_ptevac = None
        t_bcast_evac = None
        t_exp = {}
        t_stt = {}
        t_spill = {}
        pend = None
        t_xq2mm = None

        def flush_pend():
            nonlocal pend, w45, t_xq2mm
            if pend is None:
                return
            ip, tvq = pend
            pend = None
            wq2 = [tvq] + w45
            for cd in range(ND):
                Tn(lambda e, cd=cd: e.matmul(
                    ps[:, 2048 + cd * 128: 2048 + (cd + 1) * 128],
                    xq2[:, cd * 128:(cd + 1) * 128], identB[:, :],
                    start=True, stop=True), waits=wq2 if cd == 0 else ())
            tt = P.inc_last("tensor", "tv")
            t_xq2mm = tt
            ev = Sc(lambda e, ip=ip: e.copy(
                XQ3()[:, :, 1 + ip * 128: 1 + (ip + 1) * 128],
                ps[:, 2048:3072].rearrange("p (c t) -> p c t", t=128)),
                waits=[tt])
            w45 = [ev]

        for sb in range(NBL):
            for ce in range(ND):
                wts = [t_vdone, t_kdone, q03[ce % 4]]
                if sb > 0:
                    wts.append(t_exp[sb * 4 - 1])
                pc = (ce % 4) * 512
                for cd in range(ND):
                    Tn(lambda e, ce=ce, cd=cd, sb=sb, pc=pc: e.matmul(
                        ps[:, pc:pc + 512],
                        WqB[:, cd * D + ce * 128: cd * D + ce * 128 + 128],
                        XTq[:, cd * T + sb * 512: cd * T + (sb + 1) * 512],
                        start=(cd == 0), stop=(cd == ND - 1)),
                        waits=wts if cd == 0 else ())
                tt = P.inc_last("tensor", "tv")
                q03[ce % 4] = Sc(lambda e, ce=ce, pc=pc: e.copy(
                    QT[:, ce * 512:(ce + 1) * 512], ps[:, pc:pc + 512]),
                    waits=[tt])
                pro_pump(1)
                if phases >= 5:
                    pump(PUMP_Q)
            t_qslab = P.tok("sv")

            for iq in range(4):
                i = sb * 4 + iq
                L = 128 * (i + 1)
                nkb = (L + KJ - 1) // KJ
                wts = [t_qslab]
                if i > 0:
                    wts.append(t_exp[i - 1])
                for ce in range(ND):
                    for kb in range(nkb):
                        Jk = min(KJ, L - kb * KJ)
                        Tn(lambda e, ce=ce, kb=kb, Jk=Jk, iq=iq: e.matmul(
                            ps[:, kb * KJ: kb * KJ + Jk],
                            QT[:, ce * 512 + iq * 128: ce * 512 + (iq + 1) * 128],
                            XTq[:, ce * T + kb * KJ: ce * T + kb * KJ + Jk],
                            start=(ce == 0), stop=(ce == ND - 1)),
                            waits=wts if (ce == 0 and kb == 0) else ())
                t_s = P.inc_last("tensor", "tv")
                pro_pump(1)
                if phases >= 5 and iq % 2 == 1:
                    pump(PUMP_S)
                flush_pend()
                Vc(lambda e, L=L: e.tensor_tensor(
                    ps[:, L - 128:L], ps[:, L - 128:L], maskf[:, :], OP.add), waits=[t_s])
                Vc(lambda e, L=L: e.tensor_reduce(
                    mrow[:, :], ps[:, 0:L], AX.X, OP.max, False))
                tv_m = Vc(lambda e: e.tensor_scalar(
                    negm32[:, :], mrow[:, :], -1.0 / 32.0, None, OP.mult))
                t_exp[i] = Sc(lambda e, L=L: e.activation(
                    Pbuf[:, 0:L], ps[:, 0:L], AF.Exp, bias=negm32[:, 0:1],
                    scale=1.0 / 32.0, accum_out=lsum[:, 0:1]),
                    waits=[tv_m])
                tv_li = Vc(lambda e: e.reciprocal(linv[:, :], lsum[:, :]),
                           waits=[t_exp[i]])
                for g in range(0, i + 1, 8):
                    cnt = min(8, i + 1 - g)
                    wpt = [t_exp[i]] + w45
                    for c in range(cnt):
                        Tn(lambda e, g=g, c=c: e.matmul(
                            ps[:, 2048 + c * 128: 2048 + (c + 1) * 128],
                            Pbuf[:, (g + c) * 128:(g + c + 1) * 128], identB[:, :],
                            start=True, stop=True), waits=wpt if c == 0 else ())
                    tt = P.inc_last("tensor", "tv")
                    ev = Sc(lambda e, g=g, cnt=cnt: e.copy(
                        PT[:, g * 128:(g + cnt) * 128],
                        ps[:, 2048:2048 + cnt * 128]), waits=[tt])
                    w45 = [ev]
                t_ptevac = w45[0]
                watt = [t_ptevac]
                if i > 0:
                    watt.append(t_stt[i - 1])
                for c in range(i + 1):
                    for h in range(2):
                        Tn(lambda e, c=c, h=h, i=i: e.matmul(
                            ps[:, 2048 + h * 512: 2048 + (h + 1) * 512],
                            PT[:, c * 128:(c + 1) * 128],
                            Vtm[:, c * D + h * 512: c * D + (h + 1) * 512],
                            start=(c == 0), stop=(c == i)),
                            waits=watt if (c == 0 and h == 0) else ())
                t_att = P.inc_last("tensor", "tv")
                stx = xstage[i % 2]
                tldr = Sy(lambda e, i=i, stx=stx, row0=row0: e.dma_start(
                    stx[:, :], d_x[row0 + i * 128: row0 + (i + 1) * 128, :]),
                    waits=[t_stt[i - 2]] if i >= 2 else [t_xtq])
                pr = i % 2
                wkbg = [t_wvst] if (s == 0 and i < 2) else []
                t_stt[i] = Vc(lambda e, stx=stx: e.scalar_tensor_tensor(
                    t1buf[:, :], ps[:, 2048:3072], linv[:, 0:1], stx[:, :],
                    OP.mult, OP.add), waits=[t_att, tldr, tv_li] + wkbg)
                w45 = [t_stt[i]]
                Vc(lambda e, pr=pr: e.bn_stats(st12[pr][:, 0:6], t1buf[:, 0:512]))
                Vc(lambda e, pr=pr: e.bn_stats(st12[pr][:, 6:12], t1buf[:, 512:1024]))
                Vc(lambda e, pr=pr: e.bn_aggr(mv[pr][:, :], st12[pr][:, :]))
                tv_v = Vc(lambda e, pr=pr: e.tensor_scalar(
                    varp[pr][:, :], mv[pr][:, 1:2], EPS, None, OP.add))
                Sc(lambda e, pr=pr: e.activation(lnv[pr][:, :], varp[pr][:, :], AF.Ln),
                   waits=[tv_v])
                t_rs = Sc(lambda e, pr=pr: e.activation(
                    rs[pr][:, :], lnv[pr][:, :], AF.Exp, scale=-0.5))
                tv_nm = Vc(lambda e, pr=pr: e.scalar_tensor_tensor(
                    negmrs[pr][:, :], mv[pr][:, 0:1], -1.0, rs[pr][:, :],
                    OP.mult, OP.mult), waits=[t_rs])
                xb2 = x2buf[i % 2]
                wx2 = [tv_nm] + wkbg
                if i >= 2:
                    wx2.append(t_spill[i - 2])
                t_x2 = Sc(lambda e, xb2=xb2, pr=pr: e.activation(
                    xb2[:, :], t1buf[:, :], AF.Identity, bias=negmrs[pr][:, 0:1],
                    scale=rs[pr][:, 0:1]), waits=wx2)
                t_spill[i] = Gp(lambda e, i=i, xb2=xb2: e.dma_start(
                    d_x2[i * 128:(i + 1) * 128, :], xb2[:, :]), waits=[t_x2])
                Vc(lambda e, xb2=xb2: e.tensor_reduce(
                    amax[:, :], xb2[:, :], AX.X, OP.max, True), waits=[t_x2])
                Vc(lambda e: e.tensor_scalar(amax[:, :], amax[:, :], 1e-5, None, OP.max))
                Vc(lambda e: e.reciprocal(arec[:, :], amax[:, :]))
                Vc(lambda e: e.tensor_scalar(scol[:, :], arec[:, :], 127.0, None, OP.mult))
                Vc(lambda e, i=i: e.reciprocal(sxinvtm[:, i:i + 1], scol[:, :]))
                Vc(lambda e, xb2=xb2: e.tensor_scalar(
                    ytmp[:, :], xb2[:, :], scol[:, 0:1], MAGIC, OP.mult, OP.add),
                   waits=wkbg)
                tv_i32 = Vc(lambda e: e.tensor_scalar(i32t[:, :], ytmp[:, :], -MAGIC,
                                                      CFIX, OP.add, OP.add))
                tv_q = Sc(lambda e: e.activation(
                    xq2[:, :], i32t[:, :], AF.Identity, bias=noffbc[:, 0:1]),
                    waits=[t_xq2mm, tv_i32])
                pend = (i, tv_q)
                pro_pump(1)
                if phases >= 5 and iq % 2 == 1:
                    pump(PUMP_TILE)
            flush_pend()
            if s == 0:
                pro_drain(MEANS_STEP)
            t_sxh = Sc(lambda e, sb=sb: e.copy(
                sxhi[:, sb * 4:(sb + 1) * 4], sxinvtm[:, sb * 4:(sb + 1) * 4]),
                waits=[P.tok("vv")])
            tv_l = Vc(lambda e, sb=sb: e.scalar_tensor_tensor(
                sxlo[:, sb * 4:(sb + 1) * 4], sxhi[:, sb * 4:(sb + 1) * 4], -1.0,
                sxinvtm[:, sb * 4:(sb + 1) * 4], OP.mult, OP.add), waits=[t_sxh])
            wtr1 = [tv_l] + w45
            Tn(lambda e, sb=sb: e.matmul(ps[0:4, 2048:2176], sxhi[:, sb * 4:(sb + 1) * 4],
                                         identB[:, :], start=True, stop=False),
               waits=wtr1)
            Tn(lambda e, sb=sb: e.matmul(ps[0:4, 2048:2176], sxlo[:, sb * 4:(sb + 1) * 4],
                                         identB[:, :], start=False, stop=True), inc="tv")
            tt_t = P.tok("tv")
            t_rh = Sc(lambda e: e.copy(rstage_h[0:4, 0:128], ps[0:4, 2048:2176]),
                      waits=[tt_t])
            t_rl = Vc(lambda e: e.scalar_tensor_tensor(
                rstage_l[0:4, 0:128], rstage_h[0:4, 0:128], -1.0, ps[0:4, 2048:2176],
                OP.mult, OP.add), waits=[t_rh])
            for q in range(4):
                Tn(lambda e, q=q: e.matmul(
                    ps[:, 2048 + q * 128: 2048 + (q + 1) * 128],
                    EQb[:, q * 128:(q + 1) * 128], rstage_h[:, 0:128],
                    start=True, stop=False), waits=[t_rl] if q == 0 else ())
                Tn(lambda e, q=q: e.matmul(
                    ps[:, 2048 + q * 128: 2048 + (q + 1) * 128],
                    EQb[:, q * 128:(q + 1) * 128], rstage_l[:, 0:128],
                    start=False, stop=True))
            P.inc_last("tensor", "tv")
            ttb = P.tok("tv")
            t_bcast_evac = Vc(lambda e, sb=sb: e.tensor_scalar(
                sxinvb[:, 1 + sb * 512: 1 + (sb + 1) * 512], ps[:, 2048:2560],
                meanbc1[:, 0:1], None, OP.mult), waits=[ttb, tok_box["means"]])
            w45 = [t_bcast_evac]
            pro_pump(1)
            if phases >= 5:
                newly = []
                if sb >= 1:
                    newly.append(sb - 1)
                if sb == NBL - 1:
                    newly.append(NBL - 1)
                if newly:
                    pro_drain()
                    if t_prologue_g is None:
                        t_prologue_g = Qtok("gd")
                    for jj in newly:
                        unlock_tok[jj] = t_bcast_evac
                        ffn1_ready.extend((jj, c) for c in range(NH))
                pump(PUMP_BC)
        t_attn_v = P.tok("vv")
        t_attn_s = P.tok("sv")
        t_attn_t = P.tok("tv")

        if phases < 5:
            continue
        pump(10 ** 9, dummies=True)
        pump_drain()
        assert not ffn1_ready
        t_tt2 = {}
        t_sm2 = {}
        t_i322 = {}
        t_hq = {}
        t_ldh = {}
        t_f2mm = {}
        t_y2T_ev = {}
        t_ln2 = {}
        t_outs = {}
        for j in range(NBL):
            Vc(lambda e, j=j: e.tensor_tensor(
                mxf[:, :], maxbufs[j][:, :], negbufs[j][:, :], OP.max),
                waits=[t_treeblk[j], t_attn_t])
            t_mxhi = Sc(lambda e: e.copy(mxhi[:, :], mxf[:, :]),
                        waits=[P.tok("vv")])
            tv_ml = Vc(lambda e: e.scalar_tensor_tensor(
                mxlo[:, :], mxhi[:, :], -1.0, mxf[:, :], OP.mult, OP.add),
                waits=[t_mxhi])
            wtr = [tv_ml, P.tok("tv"), P.tok("vv"), P.tok("sv")]
            for q in range(4):
                Tn(lambda e, q=q: e.matmul(
                    ps[:, 3584 + q * 128:3584 + (q + 1) * 128],
                    mxhi[:, q * 128:(q + 1) * 128], identB[:, :], start=True, stop=False),
                    waits=wtr if q == 0 else ())
                Tn(lambda e, q=q: e.matmul(
                    ps[:, 3584 + q * 128:3584 + (q + 1) * 128],
                    mxlo[:, q * 128:(q + 1) * 128], identB[:, :], start=False, stop=True))
            tt_tr = P.inc_last("tensor", "tv")
            for q in range(4):
                Vc(lambda e, q=q: e.tensor_reduce(
                    matm[:, q:q + 1], ps[:, 3584 + q * 128:3584 + (q + 1) * 128],
                    AX.X, OP.max, False), waits=[tt_tr] if q == 0 else ())
            Vc(lambda e: e.tensor_scalar(matm[:, :], matm[:, :], 1e-5, None, OP.max))
            Vc(lambda e: e.reciprocal(shinvtm[:, 0:4], matm[:, :]))
            Vc(lambda e: e.tensor_scalar(strow[:, :], shinvtm[:, 0:4], 127.0, None,
                                         OP.mult))
            Vc(lambda e: e.reciprocal(shinvtm[:, 0:4], strow[:, :]))
            Vc(lambda e: e.tensor_scalar(
                scale2tm[:, :], shinvtm[:, 0:4], meanbc2[:, 0:1], None, OP.mult))
            t_sth = Sc(lambda e: e.copy(sthi[:, :], strow[:, :]), waits=[P.tok("vv")])
            tv_st = Vc(lambda e: e.scalar_tensor_tensor(
                stlo[:, :], sthi[:, :], -1.0, strow[:, :], OP.mult, OP.add),
                waits=[t_sth])
            Tn(lambda e: e.matmul(ps[0:4, 3584:3712], sthi[:, :], identB[:, :],
                                  start=True, stop=False), waits=[tv_st, tt_tr])
            Tn(lambda e: e.matmul(ps[0:4, 3584:3712], stlo[:, :], identB[:, :],
                                  start=False, stop=True), inc="tv")
            tt_t2 = P.tok("tv")
            t_rh = Sc(lambda e: e.copy(rstage_h[0:4, 0:128], ps[0:4, 3584:3712]),
                      waits=[tt_t2])
            t_rl = Vc(lambda e: e.scalar_tensor_tensor(
                rstage_l[0:4, 0:128], rstage_h[0:4, 0:128], -1.0, ps[0:4, 3584:3712],
                OP.mult, OP.add), waits=[t_rh])
            for q in range(4):
                Tn(lambda e, q=q: e.matmul(
                    ps[:, 3072 + q * 128: 3072 + (q + 1) * 128],
                    EQb[:, q * 128:(q + 1) * 128], rstage_h[:, 0:128],
                    start=True, stop=False), waits=[t_rl] if q == 0 else ())
                Tn(lambda e, q=q: e.matmul(
                    ps[:, 3072 + q * 128: 3072 + (q + 1) * 128],
                    EQb[:, q * 128:(q + 1) * 128], rstage_l[:, 0:128],
                    start=False, stop=True))
            P.inc_last("tensor", "tv")
            tt_bc = P.tok("tv")
            tv_sbc = Sc(lambda e: e.copy(sbc[:, :], ps[:, 3072:3584]),
                        waits=[tt_bc])

            def ldh(c):
                hb = hsr[c % 3]
                wlh = [t_spill_h[(j, c)]]
                if (j, c - 3) in t_tt2:
                    wlh.append(t_tt2[(j, c - 3)])
                elif j > 0:
                    wlh.append(t_tt2[(j - 1, NH - 3 + c)])
                else:
                    wlh.append(t_attn_t)
                    wlh.append(t_prologue_g)
                t_ldh[(j, c)] = Sy(lambda e, j=j, c=c, hb=hb: e.dma_start(
                    hb[:, :], d_hs[j, c]), waits=wlh)

            def f2_tt(c):
                whq = [tv_sbc, t_ldh[(j, c)]]
                if (j, c - 2) in t_i322:
                    whq.append(t_i322[(j, c - 2)])
                t_tt2[(j, c)] = Vc(lambda e, c=c: e.tensor_tensor(
                    qtmp[c % 2][:, :], hsr[c % 3][:, :], sbc[:, :],
                    OP.mult), waits=whq)

            def f2_magic(c):
                t_sm2[(j, c)] = Vc(lambda e, c=c: e.tensor_scalar(
                    qtmp[c % 2][:, :], qtmp[c % 2][:, :], MAGIC, None, OP.add),
                    waits=[t_tt2[(j, c)]])

            def f2_i32(c):
                w = [t_sm2[(j, c)]]
                if (j, c - 2) in t_hq:
                    w.append(t_hq[(j, c - 2)])
                t_i322[(j, c)] = Vc(lambda e, c=c: e.tensor_scalar(
                    qi32[c % 2][:, :], qtmp[c % 2][:, :], -MAGIC, CFIX,
                    OP.add, OP.add), waits=w)

            def f2_cast(c):
                w = [t_i322[(j, c)]]
                if (j, c - 3) in t_f2mm:
                    w.append(t_f2mm[(j, c - 3)])
                t_hq[(j, c)] = Sc(lambda e, c=c: e.activation(
                    hqb[c % 3][:, :], qi32[c % 2][:, :], AF.Identity,
                    bias=noffbc[:, 0:1]), waits=w)

            def f2_mm(c):
                stw = w2stage[c % 4]
                if c >= 4:
                    wst = [t_f2mm[(j, c - 4)]]
                elif j > 0:
                    wst = [t_f2mm[(j - 1, NH - 4 + c)]]
                else:
                    wst = [t_attn_t, t_prologue_g]
                t_w2st = Sy(lambda e, c=c, stw=stw: e.dma_start(
                    stw[:, :], d_w2q[c]), waits=wst)
                wmm2 = [t_hq[(j, c)], t_w2st]
                if c == 0:
                    wmm2.append(tv_sbc)
                    wmm2.append(P.tok("vv"))
                    wmm2.append(P.tok("sv"))
                for ce in range(ND):
                    Tn(lambda e, c=c, ce=ce: e.matmul(
                        ps[:, ce * 512:(ce + 1) * 512],
                        w2stage[c % 4][:, ce * 128:(ce + 1) * 128], hqb[c % 3][:, :],
                        start=(c == 0), stop=(c == NH - 1)),
                        waits=wmm2 if ce == 0 else ())
                t_f2mm[(j, c)] = P.inc_last("tensor", "tv")

            for cp in range(3):
                ldh(cp)
            for c in range(NH + 1):
                if c < NH:
                    f2_tt(c)
                    f2_magic(c)
                    if c + 3 < NH:
                        ldh(c + 3)
                if c >= 1:
                    f2_i32(c - 1)
                    f2_cast(c - 1)
                    f2_mm(c - 1)
            wy2 = [t_f2mm[(j, NH - 1)]]
            for ce in range(ND):
                t_hi = Sc(lambda e, ce=ce: e.copy(
                    hi2[:, ce * 512:(ce + 1) * 512], ps[:, ce * 512:(ce + 1) * 512]),
                    waits=wy2 if ce == 0 else ())
                Vc(lambda e, ce=ce: e.scalar_tensor_tensor(
                    lo2[:, ce * 512:(ce + 1) * 512], hi2[:, ce * 512:(ce + 1) * 512],
                    -1.0, ps[:, ce * 512:(ce + 1) * 512], OP.mult, OP.add),
                    waits=[t_hi])
            t_y2ev = [P.tok("sv"), P.tok("vv")]

            for q in range(4):
                i = j * 4 + q
                par = q % 2
                pc = 3072
                xr = x2rd[par]
                wldx = [t_spill[i]]
                if i >= 2:
                    wldx.append(t_y2T_ev[i - 2])
                else:
                    wldx.append(t_attn_t)
                tldx2 = Sy(lambda e, i=i, xr=xr: e.dma_start(
                    xr[:, :], d_x2[i * 128:(i + 1) * 128, :]), waits=wldx)
                wtt = [t_y2ev]
                if q >= 1:
                    wtt.append(t_y2T_ev[i - 1])
                for ce in range(ND):
                    Tn(lambda e, ce=ce, q=q, pc=pc: e.matmul(
                        ps[:, pc + ce * 128: pc + (ce + 1) * 128],
                        hi2[:, ce * 512 + q * 128: ce * 512 + (q + 1) * 128],
                        identB[:, :], start=True, stop=False),
                        waits=wtt if ce == 0 else ())
                    Tn(lambda e, ce=ce, q=q, pc=pc: e.matmul(
                        ps[:, pc + ce * 128: pc + (ce + 1) * 128],
                        lo2[:, ce * 512 + q * 128: ce * 512 + (q + 1) * 128],
                        identB[:, :], start=False, stop=True))
                tt_y = P.inc_last("tensor", "tv")
                ob = outb[par]
                wev2 = [tt_y, tldx2]
                if i >= 2:
                    wev2.append(t_ln2[i - 2])
                    wev2.append(t_outs[i - 2])
                t_y2T_ev[i] = Vc(lambda e, pc=pc, q=q, xr=xr, ob=ob: e.scalar_tensor_tensor(
                    ob[:, :], ps[:, pc:pc + 1024], scale2tm[:, q:q + 1], xr[:, :],
                    OP.mult, OP.add), waits=wev2, inc="vv")
                pr = q % 2
                Vc(lambda e, ob=ob, pr=pr: e.bn_stats(st12[pr][:, 0:6], ob[:, 0:512]))
                Vc(lambda e, ob=ob, pr=pr: e.bn_stats(st12[pr][:, 6:12], ob[:, 512:1024]))
                Vc(lambda e, pr=pr: e.bn_aggr(mv[pr][:, :], st12[pr][:, :]))
                tv_v2 = Vc(lambda e, pr=pr: e.tensor_scalar(
                    varp[pr][:, :], mv[pr][:, 1:2], EPS, None, OP.add))
                Sc(lambda e, pr=pr: e.activation(lnv[pr][:, :], varp[pr][:, :], AF.Ln),
                   waits=[tv_v2])
                t_rs2 = Sc(lambda e, pr=pr: e.activation(
                    rs[pr][:, :], lnv[pr][:, :], AF.Exp, scale=-0.5))
                tv_nm2 = Vc(lambda e, pr=pr: e.scalar_tensor_tensor(
                    negmrs[pr][:, :], mv[pr][:, 0:1], -1.0, rs[pr][:, :],
                    OP.mult, OP.mult), waits=[t_rs2])
                wout = [tv_nm2]
                if i >= 2:
                    wout.append(t_outs[i - 2])
                t_ln2[i] = Sc(lambda e, ob=ob, pr=pr: e.activation(
                    ob[:, :], ob[:, :], AF.Identity, bias=negmrs[pr][:, 0:1],
                    scale=rs[pr][:, 0:1]), waits=wout)
                t_outs[i] = Gp(lambda e, i=i, ob=ob, row0=row0: e.dma_start(
                    d_y[row0 + i * 128: row0 + (i + 1) * 128, :], ob[:, :]),
                    waits=[t_ln2[i]])

    pro_drain()

    P.op("gpsimd", lambda e: e.nop(), waits=Qtok("gd"))
    P.op("sync", lambda e: e.nop(), waits=Qtok("ld"))

    with es:
        with nc.Block() as block:
            P.emit(block, sems)
    return nc


_program_cache = {}


def _get_program(T, nseq):
    key = (T, nseq)
    if key not in _program_cache:
        _program_cache[key] = build_program(T, nseq)
    return _program_cache[key]


def _consts():
    identb = np.eye(128, dtype=ml_dtypes.bfloat16)
    e0b = np.zeros((128, 128), dtype=ml_dtypes.bfloat16)
    e0b[0, :] = 1
    eqb = np.zeros((128, 512), dtype=ml_dtypes.bfloat16)
    for q in range(4):
        eqb[q, q * 128:(q + 1) * 128] = 1
    onesb = np.ones((128, 1), dtype=ml_dtypes.bfloat16)
    maskf = np.where(np.arange(128)[None, :] > np.arange(128)[:, None],
                     np.float32(NEG_BIG), np.float32(0.0)).astype(np.float32)
    return identb, e0b, eqb, onesb, maskf


def kernel(L0_attn_x, L0_attn_Wq, L0_attn_Wk, L0_attn_Wv,
           L0_ffn_w1, L0_ffn_conv_w, L0_ffn_conv_b, L0_ffn_w2):
    x = np.ascontiguousarray(np.asarray(L0_attn_x, dtype=np.float32))
    wq = np.ascontiguousarray(np.asarray(L0_attn_Wq, dtype=np.float32).T)
    wk = np.ascontiguousarray(np.asarray(L0_attn_Wk, dtype=np.float32).T)
    wv = np.ascontiguousarray(np.asarray(L0_attn_Wv, dtype=np.float32))
    w1t = np.ascontiguousarray(np.asarray(L0_ffn_w1, dtype=np.float32).T)
    w2t = np.ascontiguousarray(np.asarray(L0_ffn_w2, dtype=np.float32).T)
    cw = np.ascontiguousarray(
        np.asarray(L0_ffn_conv_w, dtype=np.float32).reshape(NH, 128, 3)
        .transpose(1, 0, 2).reshape(128, NH * 3))
    cb = np.ascontiguousarray(
        np.asarray(L0_ffn_conv_b, dtype=np.float32).reshape(NH, 128).T)
    identb, e0b, eqb, onesb, maskf = _consts()

    B, T, _ = x.shape
    nseq = B // N_CORES
    nc = _get_program(T, nseq)
    in_maps = []
    for c in range(N_CORES):
        in_maps.append({
            "x": np.ascontiguousarray(x[c * nseq:(c + 1) * nseq].reshape(nseq * T, D)),
            "wq": wq, "wk": wk, "wv": wv, "w1t": w1t, "w2t": w2t,
            "cw": cw, "cb": cb, "identb": identb, "e0b": e0b, "eqb": eqb,
            "onesb": onesb, "maskf": maskf,
        })
    res = run_bass_kernel_spmd(nc, in_maps, core_ids=list(range(N_CORES)))
    out = np.concatenate(
        [r["y"].reshape(nseq, T, D) for r in res.results], axis=0)
    return out.astype(np.float32)


# version 29
# speedup vs baseline: 1.0226x; 1.0099x over previous
import numpy as np
import ml_dtypes
from contextlib import ExitStack

import concourse.bass as bass
import concourse.mybir as mybir
from concourse.bass_utils import run_bass_kernel_spmd

F32 = mybir.dt.float32
BF16 = mybir.dt.bfloat16
I32 = mybir.dt.int32
AF = mybir.ActivationFunctionType
OP = mybir.AluOpType
AX = mybir.AxisListType

MAGIC = 12582912.0
OFF = 256.0
CFIX = OFF + 0.5 - 2.0 ** -15
CLIP15 = 1.5 - 2.0 ** -22
NEG_BIG = -1.0e30
EPS = 1e-5

D = 1024
HID = 4096
ND = D // 128
NH = HID // 128
N_CORES = 8


class Prog:
    ENGS = ("sync", "gpsimd", "vector", "scalar", "tensor")
    AUTO = {"vector": "vv", "scalar": "sv"}

    def __init__(self):
        self.streams = {e: [] for e in self.ENGS}
        self.cnt = {}

    @staticmethod
    def _flat(waits):
        out = []
        for t in waits:
            if t is None:
                continue
            if isinstance(t, list):
                out.extend(x for x in t if x is not None)
            else:
                out.append(t)
        return out

    def op(self, eng, fn, waits=(), inc=None, inc_n=1):
        w = [(s, v) for (s, v) in self._flat(waits) if v > 0]
        if eng in self.AUTO:
            sem = self.AUTO[eng]
            assert inc is None or inc == sem
            idx = self.cnt.get(sem, 0) + 1
            self.cnt[sem] = idx
            chain = (sem, idx - 1)
            self.streams[eng].append((fn, w, sem, 1, chain))
            return (sem, idx)
        self.streams[eng].append((fn, w, inc, inc_n, None))
        if inc is not None:
            self.cnt[inc] = self.cnt.get(inc, 0) + inc_n
            return (inc, self.cnt[inc])
        return None

    def tok(self, sem):
        return (sem, self.cnt.get(sem, 0))

    def inc_last(self, eng, sem, n=1):
        fn, w, inc, inc_n, chain = self.streams[eng][-1]
        assert inc is None, "last op already has an inc"
        self.streams[eng][-1] = (fn, w, sem, n, chain)
        self.cnt[sem] = self.cnt.get(sem, 0) + n
        return (sem, self.cnt[sem])

    def emit(self, block, sems):
        for eng in self.ENGS:
            recs = self.streams[eng]

            def body(e, recs=recs, eng=eng):
                seen = {}
                for fn, waits, inc, inc_n, chain in recs:
                    for s, v in waits:
                        if v > seen.get(s, 0):
                            e.wait_ge(sems[s], v)
                            seen[s] = v
                    ins = fn(e)
                    if chain is not None and chain[1] > seen.get(chain[0], 0):
                        ins._wait_ge(sems[chain[0]], chain[1])
                        seen[chain[0]] = chain[1]
                    if inc is not None:
                        ins.then_inc(sems[inc], inc_n)
            getattr(block, eng)(body)


def build_program(T=2048, nseq=4, phases=5):
    NT = T // 128
    NBL = T // 512
    XQW = T + 2

    nc = bass.Bass("TRN2", target_bir_lowering=False)

    d_x = nc.dram_tensor("x", [nseq * T, D], F32, kind="ExternalInput").ap()
    d_y = nc.dram_tensor("y", [nseq * T, D], F32, kind="ExternalOutput").ap()
    d_wq = nc.dram_tensor("wq", [D, D], F32, kind="ExternalInput").ap()
    d_wk = nc.dram_tensor("wk", [D, D], F32, kind="ExternalInput").ap()
    d_wv = nc.dram_tensor("wv", [D, D], F32, kind="ExternalInput").ap()
    d_w1t = nc.dram_tensor("w1t", [D, HID], F32, kind="ExternalInput").ap()
    d_w2t = nc.dram_tensor("w2t", [HID, D], F32, kind="ExternalInput").ap()
    d_cw = nc.dram_tensor("cw", [128, NH * 3], F32, kind="ExternalInput").ap()
    d_cb = nc.dram_tensor("cb", [128, NH], F32, kind="ExternalInput").ap()
    d_id = nc.dram_tensor("identb", [128, 128], BF16, kind="ExternalInput").ap()
    d_e0 = nc.dram_tensor("e0b", [128, 128], BF16, kind="ExternalInput").ap()
    d_eq = nc.dram_tensor("eqb", [128, 512], BF16, kind="ExternalInput").ap()
    d_ones = nc.dram_tensor("onesb", [128, 1], BF16, kind="ExternalInput").ap()
    d_mask = nc.dram_tensor("maskf", [128, 128], F32, kind="ExternalInput").ap()
    d_wkb = nc.dram_tensor("wkb_s", [128, ND * D], BF16, kind="Internal").ap()
    d_wvb = nc.dram_tensor("wvb_s", [128, ND * D], BF16, kind="Internal").ap()
    d_w1q = nc.dram_tensor("w1q_s", [NH, 128, ND * 128], BF16, kind="Internal").ap()
    d_w2q = nc.dram_tensor("w2q_s", [NH, 128, ND * 128], BF16, kind="Internal").ap()
    d_x2 = nc.dram_tensor("x2_s", [T, D], F32, kind="Internal").ap()
    d_hs = nc.dram_tensor("hs_s", [NBL, NH, 128, 512], F32, kind="Internal").ap()
    d_sxh = nc.dram_tensor("sxh_s", [T], BF16, kind="Internal").ap()
    d_sxl = nc.dram_tensor("sxl_s", [T], BF16, kind="Internal").ap()
    d_shh = nc.dram_tensor("shh_s", [512], BF16, kind="Internal").ap()
    d_shl = nc.dram_tensor("shl_s", [512], BF16, kind="Internal").ap()

    es = ExitStack()

    def SB(name, shape, dt):
        return es.enter_context(nc.sbuf_tensor(name, shape, dt))

    identB = SB("identB", [128, 128], BF16)
    E0b = SB("E0b", [128, 128], BF16)
    EQb = SB("EQb", [128, 512], BF16)
    onesb = SB("onesb_sb", [128, 1], BF16)
    maskf = SB("maskf_sb", [128, 128], F32)
    convw = SB("convw", [128, NH * 3], F32)
    convb = SB("convb", [128, NH], F32)
    meanbc1 = SB("meanbc1", [128, 1], F32)
    meanbc2 = SB("meanbc2", [128, 1], F32)
    sq1bc = SB("sq1bc", [128, 1], F32)
    sq2bc = SB("sq2bc", [128, 1], F32)
    sxinvtm = SB("sxinvtm", [128, NT], F32)
    sxhi = SB("sxhi", [128, NT], BF16)
    sxlo = SB("sxlo", [128, NT], BF16)
    mrow = SB("mrow", [128, 1], F32)
    negm32 = SB("negm32", [128, 1], F32)
    lsum = SB("lsum", [128, 1], F32)
    linv = SB("linv", [128, 1], F32)
    st12 = [SB("st12a", [128, 12], F32), SB("st12b", [128, 12], F32)]
    mv = [SB("mva", [128, 2], F32), SB("mvb", [128, 2], F32)]
    varp = [SB("varpa", [128, 1], F32), SB("varpb", [128, 1], F32)]
    lnv = [SB("lnva", [128, 1], F32), SB("lnvb", [128, 1], F32)]
    rs = [SB("rsa", [128, 1], F32), SB("rsb", [128, 1], F32)]
    negmrs = [SB("negmrsa", [128, 1], F32), SB("negmrsb", [128, 1], F32)]
    amax = SB("amax", [128, 1], F32)
    arec = SB("arec", [128, 1], F32)
    scol = SB("scol", [128, 1], F32)
    matm = SB("matm", [128, 4], F32)
    strow = SB("strow", [128, 4], F32)
    sthi = SB("sthi", [128, 4], BF16)
    stlo = SB("stlo", [128, 4], BF16)
    shinvtm = SB("shinvtm", [128, 4], F32)
    scale2tm = SB("scale2tm", [128, 4], F32)
    psums1 = SB("psums1", [128, 16], F32)
    psums2 = SB("psums2", [128, 16], F32)
    redtmp = SB("redtmp", [128, 1], F32)
    redhi = SB("redhi", [128, 1], BF16)
    redlo = SB("redlo", [128, 1], BF16)
    sc11 = SB("sc11", [1, 1], F32)
    schi = SB("schi", [128, 1], BF16)
    sclo = SB("sclo", [128, 1], BF16)
    magicbc = SB("magicbc", [128, 1], F32)
    noffbc = SB("noffbc", [128, 1], F32)
    rstage_h = SB("rstage_h", [128, 512], BF16)
    rstage_l = SB("rstage_l", [128, 512], BF16)
    XQ2T = SB("XQ2T", [128, ND * XQW], BF16)
    sxinvb = SB("sxinvb", [128, T + 2], F32)
    WqB = SB("WqB", [128, ND * D], BF16)

    base = (nc.sbuf_base + 63) & ~63

    def AT(name, shape, dt, off):
        return nc.alloc_sbuf_tensor_at(name, shape, dt, offset=base + off)

    XTq = AT("XTq", [128, ND * T], BF16, 0)
    oL = ND * T * 2
    w1stage = [AT(f"w1s{k}", [128, ND * 128], BF16, oL + 2048 * k) for k in range(3)]
    hsf = [AT("hsf0", [128, 514], F32, oL + 6144),
           AT("hsf1", [128, 514], F32, oL + 8256)]
    ctmp = [AT("ctmp0", [128, 512], F32, oL + 10368),
            AT("ctmp1", [128, 512], F32, oL + 12416)]
    maxbufs = [AT(f"maxb{k}", [128, 512], F32, oL + 14464 + 2048 * k)
               for k in range(NBL)]
    negbufs = [AT(f"negb{k}", [128, 512], F32, oL + 14464 + 2048 * NBL + 2048 * k)
               for k in range(NBL)]
    assert 14464 + 4096 * NBL <= 32768
    Vtm = AT("Vtm", [128, NT * D], BF16, oL + 32768)
    o4 = oL + 32768 + NT * D * 2
    QT = AT("QT", [128, ND * 512], BF16, o4)
    Wv2 = AT("Wv2", [128, ND * D], BF16, o4)
    Pbuf = AT("Pbuf", [128, T], BF16, o4 + 8192)
    PT = AT("PT", [128, T], BF16, o4 + 8192 + T * 2)
    o5 = o4 + max(16384, 8192 + 2 * T * 2)
    xstage = [AT("xstage0", [128, D], F32, o5),
              AT("xstage1", [128, D], F32, o5 + 4096)]
    o6 = o5 + 8192
    t1buf = AT("t1buf", [128, D], F32, o6)
    x2buf = [AT("x2buf0", [128, D], F32, o6 + 4096),
             AT("x2buf1", [128, D], F32, o6 + 8192)]
    ytmp = AT("ytmp", [128, D], F32, o6 + 12288)
    i32t = AT("i32t", [128, D], I32, o6 + 16384)
    xq2 = AT("xq2", [128, D], BF16, o6 + 20480)
    WkB = AT("WkB", [128, ND * D], BF16, o6)
    xbstage = [AT("xbstage0", [128, D], BF16, o6 + 16384),
               AT("xbstage1", [128, D], BF16, o6 + 18432)]
    region_end_a = o6 + 22528

    oF = o4
    hsr = [AT(f"hsr{k}", [128, 512], F32, oF + 2048 * k) for k in range(3)]
    qtmp = [AT("qtmp0", [128, 512], F32, oF + 6144),
            AT("qtmp1", [128, 512], F32, oF + 8192)]
    qi32 = [AT("qi32_0", [128, 512], I32, oF + 10240),
            AT("qi32_1", [128, 512], I32, oF + 12288)]
    hqb = [AT("hq0", [128, 512], BF16, oF + 14336),
           AT("hq1", [128, 512], BF16, oF + 15360),
           AT("hq2", [128, 512], BF16, oF + 30720)]
    w2stage = [AT(f"w2s{k}", [128, ND * 128], BF16, oF + 16384 + 2048 * k)
               for k in range(4)]
    mxf = AT("mxf", [128, 512], F32, oF + 24576)
    mxhi = AT("mxhi", [128, 512], BF16, oF + 26624)
    mxlo = AT("mxlo", [128, 512], BF16, oF + 27648)
    sbc = AT("sbc", [128, 512], F32, oF + 28672)
    assert oF + 31744 <= region_end_a
    oV = oL + 32768 if NT * D * 2 >= 32768 else region_end_a
    hi2 = AT("hi2", [128, ND * 512], BF16, oV)
    lo2 = AT("lo2", [128, ND * 512], BF16, oV + 8192)
    x2rd = [AT("x2rd0", [128, D], F32, oV + 16384),
            AT("x2rd1", [128, D], F32, oV + 20480)]
    outb = [AT("outb0", [128, D], F32, oV + 24576),
            AT("outb1", [128, D], F32, oV + 28672)]
    region_end_a = max(region_end_a, oV + 32768)
    pstgh = [AT("pstgh0", [128, 2048], F32, oL),
             AT("pstgh1", [128, 2048], F32, oL + 8192)]
    pi32h = [AT("pi32h0", [128, 2048], I32, oL),
             AT("pi32h1", [128, 2048], I32, oL + 8192)]
    pternh = [AT("pternh0", [128, 2048], BF16, oL + 16384),
              AT("pternh1", [128, 2048], BF16, oL + 20480)]
    assert base + region_end_a <= nc.sbuf_top, (
        f"SBUF overflow: {base}+{region_end_a} > {nc.sbuf_top}")

    ps = es.enter_context(nc.psum_tensor("ps", [128, 4096], F32))

    NLANE = 4
    sem_names = (["tv", "vv", "sv"]
                 + [f"ld{i}" for i in range(NLANE)]
                 + [f"gd{i}" for i in range(NLANE)])
    sems = {n: es.enter_context(nc.semaphore(n)) for n in sem_names}
    P = Prog()
    _rr = {"ld": 0, "gd": 0}

    def _dma(eng, queue, fn, waits):
        lane = f"{queue}{_rr[queue] % NLANE}"
        _rr[queue] += 1
        w = list(waits) + [P.tok(lane)]
        return P.op(eng, fn, w, lane, 16)

    def Sy(fn, waits=()):
        return _dma("sync", "ld", fn, waits)

    def Gp(fn, waits=()):
        return _dma("gpsimd", "gd", fn, waits)

    def Qtok(queue):
        return [P.tok(f"{queue}{i}") for i in range(NLANE)]

    def Tn(fn, waits=(), inc=None):
        return P.op("tensor", fn, waits, inc, 1)

    def Vc(fn, waits=(), inc=None):
        return P.op("vector", fn, waits, inc, 1)

    def Sc(fn, waits=(), inc=None):
        return P.op("scalar", fn, waits, inc, 1)

    def XQ3(cd_count=ND):
        return XQ2T[:, :].rearrange("p (c w) -> p c w", w=XQW)

    Sy(lambda e: e.dma_start(identB[:, :], d_id))
    Sy(lambda e: e.dma_start(E0b[:, :], d_e0))
    Sy(lambda e: e.dma_start(EQb[:, :], d_eq))
    Sy(lambda e: e.dma_start(onesb[:, :], d_ones))
    Sy(lambda e: e.dma_start(maskf[:, :], d_mask))
    Sy(lambda e: e.dma_start(convw[:, :], d_cw))
    Sy(lambda e: e.dma_start(convb[:, :], d_cb))
    t_const = Qtok("ld")
    Vc(lambda e: e.memset(XQ3()[:, :, 0], 0.0), waits=[])
    Vc(lambda e: e.memset(XQ3()[:, :, XQW - 1], 0.0))
    Vc(lambda e: e.memset(sxinvb[:, 0:1], 0.0))
    Vc(lambda e: e.memset(sxinvb[:, T + 1:T + 2], 0.0))
    Vc(lambda e: e.memset(rstage_h[:, :], 0.0))
    Vc(lambda e: e.memset(rstage_l[:, :], 0.0))
    Vc(lambda e: e.memset(magicbc[:, :], MAGIC))
    Vc(lambda e: e.memset(noffbc[:, :], -OFF))
    Vc(lambda e: e.memset(schi[:, :], 0.0))
    Vc(lambda e: e.memset(sclo[:, :], 0.0), inc="vv")
    t_guards = P.tok("vv")

    def _w3(dap):
        return dap.rearrange("(c p) e -> p c e", p=128)

    def _s3(sap):
        return sap.rearrange("p (c e) -> p c e", e=D)

    tg_wq = Gp(lambda e: e.dma_start(_s3(WkB[:, :]), _w3(d_wq)))
    tg_wk = Gp(lambda e: e.dma_start(_s3(Wv2[:, :]), _w3(d_wk)))
    t_mev = {}
    for i_m in range(ND):
        for jh in range(2):
            wmb = [tg_wq, tg_wk]
            if (i_m - 1, jh) in t_mev:
                wmb.append(t_mev[(i_m - 1, jh)])
            for ce in range(ND):
                Tn(lambda e, i_m=i_m, jh=jh, ce=ce: e.matmul(
                    ps[:, 2048 + jh * 512: 2048 + (jh + 1) * 512],
                    WkB[:, ce * D + i_m * 128: ce * D + i_m * 128 + 128],
                    Wv2[:, ce * D + jh * 512: ce * D + (jh + 1) * 512],
                    start=(ce == 0), stop=(ce == ND - 1)),
                    waits=wmb if ce == 0 else ())
            tmb = P.inc_last("tensor", "tv")
            t_mev[(i_m, jh)] = Vc(lambda e, i_m=i_m, jh=jh: e.tensor_copy(
                WqB[:, i_m * D + jh * 512: i_m * D + (jh + 1) * 512],
                ps[:, 2048 + jh * 512: 2048 + (jh + 1) * 512]), waits=[tmb])
    P.op("vector", lambda e: e.nop(), inc="vv")
    t_mdone = P.tok("vv")
    tg_wv = Gp(lambda e: e.dma_start(_s3(WkB[:, :]), _w3(d_wv)),
               waits=[P.tok("tv")])
    t_wvst = Gp(lambda e: e.dma_start(d_wvb, WkB[:, :]), waits=[tg_wv])

    def _ld1(e, hk, st):
        cd, k = hk // 2, hk % 2
        return e.dma_start(st[:, :],
                           d_w1t[cd * 128:(cd + 1) * 128, k * 2048:(k + 1) * 2048])

    def _ld2(e, fk, st):
        r0 = fk * 256
        return e.dma_start(st[:, :].rearrange("p (g e) -> p g e", e=D),
                           d_w2t[r0:r0 + 256, :].rearrange("(g p) e -> p g e", p=128))

    pro_steps = []
    tql = {}

    def _p1w1(hk):
        st = pstgh[hk % 2]
        w = [tql[hk - 2]] if hk >= 2 else []
        tld = Sy(lambda e, hk=hk, st=st: _ld1(e, hk, st), waits=w)
        tql[hk] = Sc(lambda e, hk=hk, st=st: e.activation(
            st[:, :], st[:, :], AF.Abs, accum_out=psums1[:, hk:hk + 1]),
            waits=[tld])

    def _p1w2(fk):
        st = pstgh[fk % 2]
        tld = Gp(lambda e, fk=fk, st=st: _ld2(e, fk, st),
                 waits=[tql[16 + fk - 2]] if fk >= 2 else [tql[15]])
        tql[16 + fk] = Sc(lambda e, fk=fk, st=st: e.activation(
            st[:, :], st[:, :], AF.Abs, accum_out=psums2[:, fk:fk + 1]),
            waits=[tld])

    for _hk in range(16):
        pro_steps.append(lambda hk=_hk: _p1w1(hk))
    for _fk in range(16):
        pro_steps.append(lambda fk=_fk: _p1w2(fk))

    def mean_pipeline(psumsbuf, n, scale, meanbc, sqbc, w0=()):
        Vc(lambda e: e.tensor_reduce(redtmp[:, :], psumsbuf[:, 0:n], AX.X, OP.add, False),
           waits=w0)
        Vc(lambda e: e.tensor_copy(redhi[:, :], redtmp[:, :]))
        Vc(lambda e: e.scalar_tensor_tensor(redlo[:, :], redhi[:, :], -1.0, redtmp[:, :],
                                            OP.mult, OP.add), inc="vv")
        tv1 = P.tok("vv")
        Tn(lambda e: e.matmul(ps[0:1, 3072:3073], onesb[:, :], redhi[:, :],
                              start=True, stop=False), waits=[tv1, t_const])
        Tn(lambda e: e.matmul(ps[0:1, 3072:3073], onesb[:, :], redlo[:, :],
                              start=False, stop=True), inc="tv")
        tt1 = P.tok("tv")
        Vc(lambda e: e.tensor_scalar(sc11[:, :], ps[0:1, 3072:3073], scale, 1e-5,
                                     OP.mult, OP.max), waits=[tt1])
        Vc(lambda e: e.tensor_copy(schi[0:1, :], sc11[:, :]), waits=[t_guards])
        Vc(lambda e: e.scalar_tensor_tensor(sclo[0:1, :], schi[0:1, :], -1.0, sc11[:, :],
                                            OP.mult, OP.add), inc="vv")
        tv2 = P.tok("vv")
        Tn(lambda e: e.matmul(ps[:, 3584:3585], E0b[:, :], schi[:, :],
                              start=True, stop=False), waits=[tv2])
        Tn(lambda e: e.matmul(ps[:, 3584:3585], E0b[:, :], sclo[:, :],
                              start=False, stop=True), inc="tv")
        tt2 = P.tok("tv")
        Vc(lambda e: e.tensor_copy(meanbc[:, :], ps[:, 3584:3585]), waits=[tt2])
        Vc(lambda e: e.reciprocal(sqbc[:, :], meanbc[:, :]), inc="vv")

    tok_box = {}

    def _means_all():
        mean_pipeline(psums1, 16, 1.0 / (D * HID), meanbc1, sq1bc, w0=[tql[31]])
        mean_pipeline(psums2, 16, 1.0 / (D * HID), meanbc2, sq2bc, w0=[tql[31]])
        tok_box["means"] = P.tok("vv")

    pro_steps.append(_means_all)
    MEANS_STEP = len(pro_steps)

    def ternarize_h(par, sqbc, tld, tst_prev):
        st, pi, pt = pstgh[par], pi32h[par], pternh[par]
        Vc(lambda e, st=st: e.tensor_scalar(st[:, :], st[:, :], sqbc[:, 0:1], CLIP15,
                                            OP.mult, OP.min), waits=[tld])
        Vc(lambda e, st=st: e.tensor_scalar(st[:, :], st[:, :], -CLIP15, MAGIC,
                                            OP.max, OP.add))
        tv3 = Vc(lambda e, st=st, pi=pi: e.tensor_scalar(pi[:, :], st[:, :], -MAGIC, CFIX,
                                                         OP.add, OP.add))
        tc = Sc(lambda e, pi=pi, pt=pt: e.activation(
            pt[:, :], pi[:, :], AF.Identity, bias=noffbc[:, 0:1]),
            waits=[tv3, tst_prev])
        return tv3, tc

    tqs = {}

    def _p2w2(fk):
        par = fk % 2
        w = [tok_box["means"]]
        if fk >= 2:
            w.extend(tqs[fk - 2][:2])
        tld = Gp(lambda e, fk=fk, par=par: _ld2(e, fk, pstgh[par]), waits=w)
        tv3, tc = ternarize_h(par, sq2bc, tld,
                              tqs[fk - 2][2] if fk >= 2 else None)
        tst = Gp(lambda e, fk=fk, par=par: e.dma_start(
            d_w2q[2 * fk:2 * fk + 2].transpose([1, 0, 2]),
            pternh[par][:, :].rearrange("p (g e) -> p g e", e=D)), waits=[tc])
        tqs[fk] = (tv3, tc, tst)

    def _p2w1(hk):
        par = hk % 2
        gk = 16 + hk
        w = list(tqs[gk - 2][:2])
        tld = Sy(lambda e, hk=hk, par=par: _ld1(e, hk, pstgh[par]), waits=w)
        tv3, tc = ternarize_h(par, sq1bc, tld, tqs[gk - 2][2])
        cd, k = hk // 2, hk % 2
        tst = Gp(lambda e, cd=cd, k=k, par=par: e.dma_start(
            d_w1q[k * 16:(k + 1) * 16, :, cd * 128:(cd + 1) * 128].transpose([1, 0, 2]),
            pternh[par][:, :].rearrange("p (c h) -> p c h", h=128)), waits=[tc])
        tqs[gk] = (tv3, tc, tst)

    for _fk in range(16):
        pro_steps.append(lambda fk=_fk: _p2w2(fk))
    for _hk in range(16):
        pro_steps.append(lambda hk=_hk: _p2w1(hk))

    pro_idx = [0]

    def pro_pump(n):
        k = 0
        while pro_idx[0] < len(pro_steps) and k < n:
            pro_steps[pro_idx[0]]()
            pro_idx[0] += 1
            k += 1

    def pro_drain(upto=None):
        tgt = len(pro_steps) if upto is None else upto
        while pro_idx[0] < tgt:
            pro_steps[pro_idx[0]]()
            pro_idx[0] += 1

    t_prologue_g = None

    PUMP_Q, PUMP_S, PUMP_TILE, PUMP_BC = 1, 0, 10, 2
    ffn1_ready = []
    unlock_tok = {}
    gck = [0]
    g_hmm, g_mainev, g_conv, g_tree, g_spill = {}, {}, {}, {}, {}
    g_evLR = {}
    g_sil = {}
    t_spill_h = {}
    t_treeblk = {}

    def emit_chunk(jj, c, dummies):
        gc = gck[0]
        gck[0] += 1
        t0 = jj * 512
        stw = w1stage[gc % 3]
        hsb = hsf[gc % 2]
        ct = ctmp[gc % 2]
        wld = [g_hmm[gc - 3]] if gc >= 3 else [t_prologue_g]
        tld = Sy(lambda e, c=c, stw=stw: e.dma_start(stw[:, :], d_w1q[c]), waits=wld)
        ul = unlock_tok[jj]
        whL = [tld, ul, g_evLR.get(gc - 1)]
        for cd in range(ND):
            Tn(lambda e, cd=cd, stw=stw, t0=t0: e.matmul(
                ps[:, 3072:3073], stw[:, cd * 128:(cd + 1) * 128],
                XQ2T[:, cd * XQW + t0: cd * XQW + t0 + 1],
                start=(cd == 0), stop=(cd == ND - 1)), waits=whL if cd == 0 else ())
        for cd in range(ND):
            Tn(lambda e, cd=cd, stw=stw, t0=t0: e.matmul(
                ps[:, 3073:3074], stw[:, cd * 128:(cd + 1) * 128],
                XQ2T[:, cd * XQW + t0 + 513: cd * XQW + t0 + 514],
                start=(cd == 0), stop=(cd == ND - 1)))
        t_hLR = P.inc_last("tensor", "tv")
        wmn = [g_mainev.get(gc - 1)] if gc >= 1 else [tld, ul]
        for cd in range(ND):
            Tn(lambda e, cd=cd, stw=stw, t0=t0: e.matmul(
                ps[:, 3584:4096], stw[:, cd * 128:(cd + 1) * 128],
                XQ2T[:, cd * XQW + 1 + t0: cd * XQW + 1 + t0 + 512],
                start=(cd == 0), stop=(cd == ND - 1)),
                waits=wmn if cd == 0 else ())
        g_hmm[gc] = P.inc_last("tensor", "tv")
        Vc(lambda e, hsb=hsb, t0=t0: e.tensor_tensor(
            hsb[:, 0:1], ps[:, 3072:3073], sxinvb[:, t0:t0 + 1], OP.mult),
            waits=[t_hLR, ul, g_conv.get(gc - 2)])
        g_evLR[gc] = Vc(lambda e, hsb=hsb, t0=t0: e.tensor_tensor(
            hsb[:, 513:514], ps[:, 3073:3074], sxinvb[:, t0 + 513:t0 + 514],
            OP.mult))
        if dummies:
            Tn(lambda e: e.matmul(ps[0:64, 0:64], identB[:, 0:64], identB[:, 0:64],
                                  start=True, stop=True))
            Tn(lambda e: e.matmul(ps[0:64, 0:64], identB[:, 0:64], identB[:, 0:64],
                                  start=True, stop=True))
        g_mainev[gc] = Vc(lambda e, hsb=hsb, t0=t0: e.tensor_tensor(
            hsb[:, 1:513], ps[:, 3584:4096], sxinvb[:, 1 + t0:1 + t0 + 512],
            OP.mult), waits=[g_hmm[gc]])
        return (jj, c, gc)

    def chunk_mid(st):
        jj, c, gc = st
        hsb = hsf[gc % 2]
        ct = ctmp[gc % 2]
        t_tap = Sc(lambda e, ct=ct, hsb=hsb, c=c: e.activation(
            ct[:, :], hsb[:, 0:512], AF.Copy, scale=convw[:, 3 * c:3 * c + 1]),
            waits=[g_mainev[gc], g_spill.get(gc - 2), g_tree.get(gc - 2)])
        Vc(lambda e, ct=ct, hsb=hsb, c=c: e.scalar_tensor_tensor(
            ct[:, :], hsb[:, 1:513], convw[:, 3 * c + 1:3 * c + 2], ct[:, :],
            OP.mult, OP.add), waits=[t_tap])
        g_conv[gc] = Vc(lambda e, ct=ct, hsb=hsb, c=c: e.scalar_tensor_tensor(
            ct[:, :], hsb[:, 2:514], convw[:, 3 * c + 2:3 * c + 3], ct[:, :],
            OP.mult, OP.add))
        g_sil[gc] = Sc(lambda e, ct=ct, c=c: e.activation(
            ct[:, :], ct[:, :], AF.Silu, bias=convb[:, c:c + 1]),
            waits=[g_conv[gc]])

    def chunk_back(st):
        jj, c, gc = st
        ct = ctmp[gc % 2]
        t_sil = g_sil[gc]
        g_spill[gc] = t_spill_h[(jj, c)] = Gp(
            lambda e, jj=jj, c=c, ct=ct: e.dma_start(d_hs[jj, c], ct[:, :]),
            waits=[t_sil])
        mb, nb = maxbufs[jj], negbufs[jj]
        if c == 0:
            Vc(lambda e, mb=mb, ct=ct: e.tensor_copy(mb[:, :], ct[:, :]),
               waits=[t_sil, t_prologue_g])
            g_tree[gc] = t_treeblk[jj] = Vc(
                lambda e, nb=nb, ct=ct: e.tensor_scalar(
                    nb[:, :], ct[:, :], -1.0, None, OP.mult))
        else:
            Vc(lambda e, mb=mb, ct=ct: e.tensor_tensor(
                mb[:, :], mb[:, :], ct[:, :], OP.max), waits=[t_sil])
            g_tree[gc] = t_treeblk[jj] = Vc(
                lambda e, nb=nb, ct=ct: e.scalar_tensor_tensor(
                    nb[:, :], ct[:, :], -1.0, nb[:, :], OP.mult, OP.max))

    fstates = []
    stage_idx = [0, 0]

    def pump(n, dummies=False):
        k = 0
        while ffn1_ready and k < n:
            jj, c = ffn1_ready.pop(0)
            fstates.append(emit_chunk(jj, c, dummies))
            if stage_idx[0] < len(fstates) - 1:
                chunk_mid(fstates[stage_idx[0]])
                stage_idx[0] += 1
            if stage_idx[1] < stage_idx[0] - 1:
                chunk_back(fstates[stage_idx[1]])
                stage_idx[1] += 1
            k += 1

    def pump_drain():
        while stage_idx[0] < len(fstates) or stage_idx[1] < len(fstates):
            if stage_idx[0] < len(fstates):
                chunk_mid(fstates[stage_idx[0]])
                stage_idx[0] += 1
            if stage_idx[1] < stage_idx[0]:
                chunk_back(fstates[stage_idx[1]])
                stage_idx[1] += 1

    KJ = 512

    for s in range(nseq if phases >= 2 else 0):
        row0 = s * T
        t_prev_t = P.tok("tv")
        t_prev_v = P.tok("vv")
        t_prev_s = P.tok("sv")
        t_prev_g = Qtok("gd")

        t_cast = {}
        t_evac = {}
        for i in range(NT):
            stx = xstage[i % 2]
            stb = xbstage[i % 2]
            if i >= 2:
                w = [t_cast[i - 2]]
            elif s == 0:
                w = []
            else:
                w = [t_prev_v, t_prev_s, t_prev_g]
            tld = Sy(lambda e, i=i, stx=stx, row0=row0: e.dma_start(
                stx[:, :], d_x[row0 + i * 128: row0 + (i + 1) * 128, :]), waits=w)
            wc = [tld]
            if i >= 2:
                wc.append(t_evac[i - 2])
            t_cast[i] = Vc(lambda e, stx=stx, stb=stb: e.tensor_copy(stb[:, :], stx[:, :]),
                           waits=wc, inc="vv")
            pc = (i % 2) * 1024
            wts = [t_cast[i], t_guards]
            if i >= 2:
                wts.append(t_evac[i - 2])
            for cd in range(ND):
                Tn(lambda e, cd=cd, pc=pc, stb=stb: e.matmul(
                    ps[:, pc + cd * 128: pc + (cd + 1) * 128],
                    stb[:, cd * 128:(cd + 1) * 128], identB[:, :],
                    start=True, stop=True), waits=wts if cd == 0 else ())
            tt = P.inc_last("tensor", "tv")
            t_evac[i] = Vc(lambda e, i=i, pc=pc: e.tensor_copy(
                XTq[:, :].rearrange("p (c t) -> p c t", t=T)[:, :, i * 128:(i + 1) * 128],
                ps[:, pc:pc + 1024].rearrange("p (c t) -> p c t", t=128)),
                waits=[tt], inc="vv")
            pro_pump(1)
        t_xtq = P.tok("vv")

        if phases < 3:
            continue
        tldv = Sy(lambda e: e.dma_start(Wv2[:, :], d_wvb),
                  waits=([t_mdone, t_wvst] if s == 0 else
                         [t_prev_t, t_prev_g, t_mdone]))
        t_kdone = P.tok("sv")
        tve = {}
        for i in range(NT):
            pc = (i % 2) * 1024
            wts = [tldv, t_xtq]
            if i >= 2:
                wts.append(tve[i - 2])
            else:
                wts.append(t_evac[NT - 1])
            for cd in range(ND):
                for h in range(2):
                    Tn(lambda e, i=i, cd=cd, h=h, pc=pc: e.matmul(
                        ps[:, pc + h * 512: pc + (h + 1) * 512],
                        XTq[:, cd * T + i * 128: cd * T + (i + 1) * 128],
                        Wv2[:, cd * D + h * 512: cd * D + (h + 1) * 512],
                        start=(cd == 0), stop=(cd == ND - 1)),
                        waits=wts if (cd == 0 and h == 0) else ())
            tt = P.inc_last("tensor", "tv")
            tve[i] = Sc(lambda e, i=i, pc=pc: e.copy(
                Vtm[:, i * D:(i + 1) * D], ps[:, pc:pc + 1024]), waits=[tt])
            pro_pump(2)
        t_vdone = P.tok("sv")

        if phases < 4:
            continue
        w45 = [t_mdone, t_prev_s, t_prev_v]
        q03 = {k: None for k in range(4)}
        t_ptevac = None
        t_bcast_evac = None
        t_exp = {}
        t_stt = {}
        t_spill = {}
        pend = None
        t_xq2mm = None

        def flush_pend():
            nonlocal pend, w45, t_xq2mm
            if pend is None:
                return
            ip, tvq = pend
            pend = None
            wq2 = [tvq] + w45
            for cd in range(ND):
                Tn(lambda e, cd=cd: e.matmul(
                    ps[:, 2048 + cd * 128: 2048 + (cd + 1) * 128],
                    xq2[:, cd * 128:(cd + 1) * 128], identB[:, :],
                    start=True, stop=True), waits=wq2 if cd == 0 else ())
            tt = P.inc_last("tensor", "tv")
            t_xq2mm = tt
            ev = Sc(lambda e, ip=ip: e.copy(
                XQ3()[:, :, 1 + ip * 128: 1 + (ip + 1) * 128],
                ps[:, 2048:3072].rearrange("p (c t) -> p c t", t=128)),
                waits=[tt])
            w45 = [ev]

        for sb in range(NBL):
            for ce in range(ND):
                wts = [t_vdone, t_kdone, q03[ce % 4]]
                if sb > 0:
                    wts.append(t_exp[sb * 4 - 1])
                pc = (ce % 4) * 512
                for cd in range(ND):
                    Tn(lambda e, ce=ce, cd=cd, sb=sb, pc=pc: e.matmul(
                        ps[:, pc:pc + 512],
                        WqB[:, cd * D + ce * 128: cd * D + ce * 128 + 128],
                        XTq[:, cd * T + sb * 512: cd * T + (sb + 1) * 512],
                        start=(cd == 0), stop=(cd == ND - 1)),
                        waits=wts if cd == 0 else ())
                tt = P.inc_last("tensor", "tv")
                q03[ce % 4] = Sc(lambda e, ce=ce, pc=pc: e.copy(
                    QT[:, ce * 512:(ce + 1) * 512], ps[:, pc:pc + 512]),
                    waits=[tt])
                pro_pump(1)
                if phases >= 5:
                    pump(PUMP_Q)
            t_qslab = P.tok("sv")

            for iq in range(4):
                i = sb * 4 + iq
                L = 128 * (i + 1)
                nkb = (L + KJ - 1) // KJ
                wts = [t_qslab]
                if i > 0:
                    wts.append(t_exp[i - 1])
                for ce in range(ND):
                    for kb in range(nkb):
                        Jk = min(KJ, L - kb * KJ)
                        Tn(lambda e, ce=ce, kb=kb, Jk=Jk, iq=iq: e.matmul(
                            ps[:, kb * KJ: kb * KJ + Jk],
                            QT[:, ce * 512 + iq * 128: ce * 512 + (iq + 1) * 128],
                            XTq[:, ce * T + kb * KJ: ce * T + kb * KJ + Jk],
                            start=(ce == 0), stop=(ce == ND - 1)),
                            waits=wts if (ce == 0 and kb == 0) else ())
                t_s = P.inc_last("tensor", "tv")
                pro_pump(1)
                if phases >= 5 and iq % 2 == 1:
                    pump(PUMP_S)
                flush_pend()
                Vc(lambda e, L=L: e.tensor_tensor(
                    ps[:, L - 128:L], ps[:, L - 128:L], maskf[:, :], OP.add), waits=[t_s])
                Vc(lambda e, L=L: e.tensor_reduce(
                    mrow[:, :], ps[:, 0:L], AX.X, OP.max, False))
                tv_m = Vc(lambda e: e.tensor_scalar(
                    negm32[:, :], mrow[:, :], -1.0 / 32.0, None, OP.mult))
                t_exp[i] = Sc(lambda e, L=L: e.activation(
                    Pbuf[:, 0:L], ps[:, 0:L], AF.Exp, bias=negm32[:, 0:1],
                    scale=1.0 / 32.0, accum_out=lsum[:, 0:1]),
                    waits=[tv_m])
                tv_li = Vc(lambda e: e.reciprocal(linv[:, :], lsum[:, :]),
                           waits=[t_exp[i]])
                for g in range(0, i + 1, 8):
                    cnt = min(8, i + 1 - g)
                    wpt = [t_exp[i]] + w45
                    for c in range(cnt):
                        Tn(lambda e, g=g, c=c: e.matmul(
                            ps[:, 2048 + c * 128: 2048 + (c + 1) * 128],
                            Pbuf[:, (g + c) * 128:(g + c + 1) * 128], identB[:, :],
                            start=True, stop=True), waits=wpt if c == 0 else ())
                    tt = P.inc_last("tensor", "tv")
                    ev = Sc(lambda e, g=g, cnt=cnt: e.copy(
                        PT[:, g * 128:(g + cnt) * 128],
                        ps[:, 2048:2048 + cnt * 128]), waits=[tt])
                    w45 = [ev]
                t_ptevac = w45[0]
                watt = [t_ptevac]
                if i > 0:
                    watt.append(t_stt[i - 1])
                for c in range(i + 1):
                    for h in range(2):
                        Tn(lambda e, c=c, h=h, i=i: e.matmul(
                            ps[:, 2048 + h * 512: 2048 + (h + 1) * 512],
                            PT[:, c * 128:(c + 1) * 128],
                            Vtm[:, c * D + h * 512: c * D + (h + 1) * 512],
                            start=(c == 0), stop=(c == i)),
                            waits=watt if (c == 0 and h == 0) else ())
                t_att = P.inc_last("tensor", "tv")
                stx = xstage[i % 2]
                tldr = Sy(lambda e, i=i, stx=stx, row0=row0: e.dma_start(
                    stx[:, :], d_x[row0 + i * 128: row0 + (i + 1) * 128, :]),
                    waits=[t_stt[i - 2]] if i >= 2 else [t_xtq])
                pr = i % 2
                wkbg = [t_wvst] if (s == 0 and i < 2) else []
                t_stt[i] = Vc(lambda e, stx=stx: e.scalar_tensor_tensor(
                    t1buf[:, :], ps[:, 2048:3072], linv[:, 0:1], stx[:, :],
                    OP.mult, OP.add), waits=[t_att, tldr, tv_li] + wkbg)
                w45 = [t_stt[i]]
                Vc(lambda e, pr=pr: e.bn_stats(st12[pr][:, 0:6], t1buf[:, 0:512]))
                Vc(lambda e, pr=pr: e.bn_stats(st12[pr][:, 6:12], t1buf[:, 512:1024]))
                Vc(lambda e, pr=pr: e.bn_aggr(mv[pr][:, :], st12[pr][:, :]))
                tv_v = Vc(lambda e, pr=pr: e.tensor_scalar(
                    varp[pr][:, :], mv[pr][:, 1:2], EPS, None, OP.add))
                Sc(lambda e, pr=pr: e.activation(lnv[pr][:, :], varp[pr][:, :], AF.Ln),
                   waits=[tv_v])
                t_rs = Sc(lambda e, pr=pr: e.activation(
                    rs[pr][:, :], lnv[pr][:, :], AF.Exp, scale=-0.5))
                tv_nm = Vc(lambda e, pr=pr: e.scalar_tensor_tensor(
                    negmrs[pr][:, :], mv[pr][:, 0:1], -1.0, rs[pr][:, :],
                    OP.mult, OP.mult), waits=[t_rs])
                xb2 = x2buf[i % 2]
                wx2 = [tv_nm] + wkbg
                if i >= 2:
                    wx2.append(t_spill[i - 2])
                t_x2 = Sc(lambda e, xb2=xb2, pr=pr: e.activation(
                    xb2[:, :], t1buf[:, :], AF.Identity, bias=negmrs[pr][:, 0:1],
                    scale=rs[pr][:, 0:1]), waits=wx2)
                t_spill[i] = Gp(lambda e, i=i, xb2=xb2: e.dma_start(
                    d_x2[i * 128:(i + 1) * 128, :], xb2[:, :]), waits=[t_x2])
                Vc(lambda e, xb2=xb2: e.tensor_reduce(
                    amax[:, :], xb2[:, :], AX.X, OP.max, True), waits=[t_x2])
                Vc(lambda e: e.tensor_scalar(amax[:, :], amax[:, :], 1e-5, None, OP.max))
                Vc(lambda e: e.reciprocal(arec[:, :], amax[:, :]))
                Vc(lambda e: e.tensor_scalar(scol[:, :], arec[:, :], 127.0, None, OP.mult))
                Vc(lambda e, i=i: e.reciprocal(sxinvtm[:, i:i + 1], scol[:, :]))
                Vc(lambda e, xb2=xb2: e.tensor_scalar(
                    ytmp[:, :], xb2[:, :], scol[:, 0:1], MAGIC, OP.mult, OP.add),
                   waits=wkbg)
                tv_i32 = Vc(lambda e: e.tensor_scalar(i32t[:, :], ytmp[:, :], -MAGIC,
                                                      CFIX, OP.add, OP.add))
                tv_q = Sc(lambda e: e.activation(
                    xq2[:, :], i32t[:, :], AF.Identity, bias=noffbc[:, 0:1]),
                    waits=[t_xq2mm, tv_i32])
                pend = (i, tv_q)
                pro_pump(1)
                if phases >= 5 and iq % 2 == 1:
                    pump(PUMP_TILE)
            flush_pend()
            if s == 0:
                pro_drain(MEANS_STEP)
            t_sxh = Sc(lambda e, sb=sb: e.copy(
                sxhi[:, sb * 4:(sb + 1) * 4], sxinvtm[:, sb * 4:(sb + 1) * 4]),
                waits=[P.tok("vv")])
            tv_l = Vc(lambda e, sb=sb: e.scalar_tensor_tensor(
                sxlo[:, sb * 4:(sb + 1) * 4], sxhi[:, sb * 4:(sb + 1) * 4], -1.0,
                sxinvtm[:, sb * 4:(sb + 1) * 4], OP.mult, OP.add), waits=[t_sxh])
            wtr1 = [tv_l] + w45
            Tn(lambda e, sb=sb: e.matmul(ps[0:4, 2048:2176], sxhi[:, sb * 4:(sb + 1) * 4],
                                         identB[:, :], start=True, stop=False),
               waits=wtr1)
            Tn(lambda e, sb=sb: e.matmul(ps[0:4, 2048:2176], sxlo[:, sb * 4:(sb + 1) * 4],
                                         identB[:, :], start=False, stop=True), inc="tv")
            tt_t = P.tok("tv")
            t_rh = Sc(lambda e: e.copy(rstage_h[0:4, 0:128], ps[0:4, 2048:2176]),
                      waits=[tt_t])
            t_rl = Vc(lambda e: e.scalar_tensor_tensor(
                rstage_l[0:4, 0:128], rstage_h[0:4, 0:128], -1.0, ps[0:4, 2048:2176],
                OP.mult, OP.add), waits=[t_rh])
            for q in range(4):
                Tn(lambda e, q=q: e.matmul(
                    ps[:, 2048 + q * 128: 2048 + (q + 1) * 128],
                    EQb[:, q * 128:(q + 1) * 128], rstage_h[:, 0:128],
                    start=True, stop=False), waits=[t_rl] if q == 0 else ())
                Tn(lambda e, q=q: e.matmul(
                    ps[:, 2048 + q * 128: 2048 + (q + 1) * 128],
                    EQb[:, q * 128:(q + 1) * 128], rstage_l[:, 0:128],
                    start=False, stop=True))
            P.inc_last("tensor", "tv")
            ttb = P.tok("tv")
            t_bcast_evac = Vc(lambda e, sb=sb: e.tensor_scalar(
                sxinvb[:, 1 + sb * 512: 1 + (sb + 1) * 512], ps[:, 2048:2560],
                meanbc1[:, 0:1], None, OP.mult), waits=[ttb, tok_box["means"]])
            w45 = [t_bcast_evac]
            pro_pump(1)
            if phases >= 5:
                newly = []
                if sb >= 1:
                    newly.append(sb - 1)
                if sb == NBL - 1:
                    newly.append(NBL - 1)
                if newly:
                    pro_drain()
                    if t_prologue_g is None:
                        t_prologue_g = Qtok("gd")
                    for jj in newly:
                        unlock_tok[jj] = t_bcast_evac
                        ffn1_ready.extend((jj, c) for c in range(NH))
                pump(PUMP_BC)
        t_attn_v = P.tok("vv")
        t_attn_s = P.tok("sv")
        t_attn_t = P.tok("tv")

        if phases < 5:
            continue
        pump(10 ** 9, dummies=True)
        pump_drain()
        assert not ffn1_ready
        t_tt2 = {}
        t_sm2 = {}
        t_i322 = {}
        t_hq = {}
        t_ldh = {}
        t_f2mm = {}
        t_y2T_ev = {}
        t_ln2 = {}
        t_outs = {}
        for j in range(NBL):
            Vc(lambda e, j=j: e.tensor_tensor(
                mxf[:, :], maxbufs[j][:, :], negbufs[j][:, :], OP.max),
                waits=[t_treeblk[j], t_attn_t])
            t_mxhi = Sc(lambda e: e.copy(mxhi[:, :], mxf[:, :]),
                        waits=[P.tok("vv")])
            tv_ml = Vc(lambda e: e.scalar_tensor_tensor(
                mxlo[:, :], mxhi[:, :], -1.0, mxf[:, :], OP.mult, OP.add),
                waits=[t_mxhi])
            wtr = [tv_ml, P.tok("tv"), P.tok("vv"), P.tok("sv")]
            for q in range(4):
                Tn(lambda e, q=q: e.matmul(
                    ps[:, 3584 + q * 128:3584 + (q + 1) * 128],
                    mxhi[:, q * 128:(q + 1) * 128], identB[:, :], start=True, stop=False),
                    waits=wtr if q == 0 else ())
                Tn(lambda e, q=q: e.matmul(
                    ps[:, 3584 + q * 128:3584 + (q + 1) * 128],
                    mxlo[:, q * 128:(q + 1) * 128], identB[:, :], start=False, stop=True))
            tt_tr = P.inc_last("tensor", "tv")
            for q in range(4):
                Vc(lambda e, q=q: e.tensor_reduce(
                    matm[:, q:q + 1], ps[:, 3584 + q * 128:3584 + (q + 1) * 128],
                    AX.X, OP.max, False), waits=[tt_tr] if q == 0 else ())
            Vc(lambda e: e.tensor_scalar(matm[:, :], matm[:, :], 1e-5, None, OP.max))
            Vc(lambda e: e.reciprocal(shinvtm[:, 0:4], matm[:, :]))
            Vc(lambda e: e.tensor_scalar(strow[:, :], shinvtm[:, 0:4], 127.0, None,
                                         OP.mult))
            Vc(lambda e: e.reciprocal(shinvtm[:, 0:4], strow[:, :]))
            Vc(lambda e: e.tensor_scalar(
                scale2tm[:, :], shinvtm[:, 0:4], meanbc2[:, 0:1], None, OP.mult))
            t_sth = Sc(lambda e: e.copy(sthi[:, :], strow[:, :]), waits=[P.tok("vv")])
            tv_st = Vc(lambda e: e.scalar_tensor_tensor(
                stlo[:, :], sthi[:, :], -1.0, strow[:, :], OP.mult, OP.add),
                waits=[t_sth])
            Tn(lambda e: e.matmul(ps[0:4, 3584:3712], sthi[:, :], identB[:, :],
                                  start=True, stop=False), waits=[tv_st, tt_tr])
            Tn(lambda e: e.matmul(ps[0:4, 3584:3712], stlo[:, :], identB[:, :],
                                  start=False, stop=True), inc="tv")
            tt_t2 = P.tok("tv")
            t_rh = Sc(lambda e: e.copy(rstage_h[0:4, 0:128], ps[0:4, 3584:3712]),
                      waits=[tt_t2])
            t_rl = Vc(lambda e: e.scalar_tensor_tensor(
                rstage_l[0:4, 0:128], rstage_h[0:4, 0:128], -1.0, ps[0:4, 3584:3712],
                OP.mult, OP.add), waits=[t_rh])
            for q in range(4):
                Tn(lambda e, q=q: e.matmul(
                    ps[:, 3072 + q * 128: 3072 + (q + 1) * 128],
                    EQb[:, q * 128:(q + 1) * 128], rstage_h[:, 0:128],
                    start=True, stop=False), waits=[t_rl] if q == 0 else ())
                Tn(lambda e, q=q: e.matmul(
                    ps[:, 3072 + q * 128: 3072 + (q + 1) * 128],
                    EQb[:, q * 128:(q + 1) * 128], rstage_l[:, 0:128],
                    start=False, stop=True))
            P.inc_last("tensor", "tv")
            tt_bc = P.tok("tv")
            tv_sbc = Sc(lambda e: e.copy(sbc[:, :], ps[:, 3072:3584]),
                        waits=[tt_bc])

            def ldh(c):
                hb = hsr[c % 3]
                wlh = [t_spill_h[(j, c)]]
                if (j, c - 3) in t_tt2:
                    wlh.append(t_tt2[(j, c - 3)])
                elif j > 0:
                    wlh.append(t_tt2[(j - 1, NH - 3 + c)])
                else:
                    wlh.append(t_attn_t)
                    wlh.append(t_prologue_g)
                t_ldh[(j, c)] = Sy(lambda e, j=j, c=c, hb=hb: e.dma_start(
                    hb[:, :], d_hs[j, c]), waits=wlh)

            def f2_tt(c):
                whq = [tv_sbc, t_ldh[(j, c)]]
                if (j, c - 2) in t_i322:
                    whq.append(t_i322[(j, c - 2)])
                t_tt2[(j, c)] = Vc(lambda e, c=c: e.tensor_tensor(
                    qtmp[c % 2][:, :], hsr[c % 3][:, :], sbc[:, :],
                    OP.mult), waits=whq)

            def f2_magic(c):
                t_sm2[(j, c)] = Vc(lambda e, c=c: e.tensor_scalar(
                    qtmp[c % 2][:, :], qtmp[c % 2][:, :], MAGIC, None, OP.add),
                    waits=[t_tt2[(j, c)]])

            def f2_i32(c):
                w = [t_sm2[(j, c)]]
                if (j, c - 2) in t_hq:
                    w.append(t_hq[(j, c - 2)])
                t_i322[(j, c)] = Vc(lambda e, c=c: e.tensor_scalar(
                    qi32[c % 2][:, :], qtmp[c % 2][:, :], -MAGIC, CFIX,
                    OP.add, OP.add), waits=w)

            def f2_cast(c):
                w = [t_i322[(j, c)]]
                if (j, c - 3) in t_f2mm:
                    w.append(t_f2mm[(j, c - 3)])
                t_hq[(j, c)] = Sc(lambda e, c=c: e.activation(
                    hqb[c % 3][:, :], qi32[c % 2][:, :], AF.Identity,
                    bias=noffbc[:, 0:1]), waits=w)

            def f2_mm(c):
                stw = w2stage[c % 4]
                if c >= 4:
                    wst = [t_f2mm[(j, c - 4)]]
                elif j > 0:
                    wst = [t_f2mm[(j - 1, NH - 4 + c)]]
                else:
                    wst = [t_attn_t, t_prologue_g]
                t_w2st = Sy(lambda e, c=c, stw=stw: e.dma_start(
                    stw[:, :], d_w2q[c]), waits=wst)
                wmm2 = [t_hq[(j, c)], t_w2st]
                if c == 0:
                    wmm2.append(tv_sbc)
                    wmm2.append(P.tok("vv"))
                    wmm2.append(P.tok("sv"))
                for ce in range(ND):
                    Tn(lambda e, c=c, ce=ce: e.matmul(
                        ps[:, ce * 512:(ce + 1) * 512],
                        w2stage[c % 4][:, ce * 128:(ce + 1) * 128], hqb[c % 3][:, :],
                        start=(c == 0), stop=(c == NH - 1)),
                        waits=wmm2 if ce == 0 else ())
                t_f2mm[(j, c)] = P.inc_last("tensor", "tv")

            for cp in range(3):
                ldh(cp)
            for c in range(NH + 1):
                if c < NH:
                    f2_tt(c)
                    f2_magic(c)
                    if c + 3 < NH:
                        ldh(c + 3)
                if c >= 1:
                    f2_i32(c - 1)
                    f2_cast(c - 1)
                    f2_mm(c - 1)
            wy2 = [t_f2mm[(j, NH - 1)]]
            for ce in range(ND):
                t_hi = Sc(lambda e, ce=ce: e.copy(
                    hi2[:, ce * 512:(ce + 1) * 512], ps[:, ce * 512:(ce + 1) * 512]),
                    waits=wy2 if ce == 0 else ())
                Vc(lambda e, ce=ce: e.scalar_tensor_tensor(
                    lo2[:, ce * 512:(ce + 1) * 512], hi2[:, ce * 512:(ce + 1) * 512],
                    -1.0, ps[:, ce * 512:(ce + 1) * 512], OP.mult, OP.add),
                    waits=[t_hi])
            t_y2ev = [P.tok("sv"), P.tok("vv")]

            for q in range(4):
                i = j * 4 + q
                par = q % 2
                pc = 3072
                xr = x2rd[par]
                wldx = [t_spill[i]]
                if i >= 2:
                    wldx.append(t_y2T_ev[i - 2])
                else:
                    wldx.append(t_attn_t)
                tldx2 = Sy(lambda e, i=i, xr=xr: e.dma_start(
                    xr[:, :], d_x2[i * 128:(i + 1) * 128, :]), waits=wldx)
                wtt = [t_y2ev]
                if q >= 1:
                    wtt.append(t_y2T_ev[i - 1])
                for ce in range(ND):
                    Tn(lambda e, ce=ce, q=q, pc=pc: e.matmul(
                        ps[:, pc + ce * 128: pc + (ce + 1) * 128],
                        hi2[:, ce * 512 + q * 128: ce * 512 + (q + 1) * 128],
                        identB[:, :], start=True, stop=False),
                        waits=wtt if ce == 0 else ())
                    Tn(lambda e, ce=ce, q=q, pc=pc: e.matmul(
                        ps[:, pc + ce * 128: pc + (ce + 1) * 128],
                        lo2[:, ce * 512 + q * 128: ce * 512 + (q + 1) * 128],
                        identB[:, :], start=False, stop=True))
                tt_y = P.inc_last("tensor", "tv")
                ob = outb[par]
                wev2 = [tt_y, tldx2]
                if i >= 2:
                    wev2.append(t_ln2[i - 2])
                    wev2.append(t_outs[i - 2])
                t_y2T_ev[i] = Vc(lambda e, pc=pc, q=q, xr=xr, ob=ob: e.scalar_tensor_tensor(
                    ob[:, :], ps[:, pc:pc + 1024], scale2tm[:, q:q + 1], xr[:, :],
                    OP.mult, OP.add), waits=wev2, inc="vv")
                pr = q % 2
                Vc(lambda e, ob=ob, pr=pr: e.bn_stats(st12[pr][:, 0:6], ob[:, 0:512]))
                Vc(lambda e, ob=ob, pr=pr: e.bn_stats(st12[pr][:, 6:12], ob[:, 512:1024]))
                Vc(lambda e, pr=pr: e.bn_aggr(mv[pr][:, :], st12[pr][:, :]))
                tv_v2 = Vc(lambda e, pr=pr: e.tensor_scalar(
                    varp[pr][:, :], mv[pr][:, 1:2], EPS, None, OP.add))
                Sc(lambda e, pr=pr: e.activation(lnv[pr][:, :], varp[pr][:, :], AF.Ln),
                   waits=[tv_v2])
                t_rs2 = Sc(lambda e, pr=pr: e.activation(
                    rs[pr][:, :], lnv[pr][:, :], AF.Exp, scale=-0.5))
                tv_nm2 = Vc(lambda e, pr=pr: e.scalar_tensor_tensor(
                    negmrs[pr][:, :], mv[pr][:, 0:1], -1.0, rs[pr][:, :],
                    OP.mult, OP.mult), waits=[t_rs2])
                wout = [tv_nm2]
                if i >= 2:
                    wout.append(t_outs[i - 2])
                t_ln2[i] = Sc(lambda e, ob=ob, pr=pr: e.activation(
                    ob[:, :], ob[:, :], AF.Identity, bias=negmrs[pr][:, 0:1],
                    scale=rs[pr][:, 0:1]), waits=wout)
                t_outs[i] = Gp(lambda e, i=i, ob=ob, row0=row0: e.dma_start(
                    d_y[row0 + i * 128: row0 + (i + 1) * 128, :], ob[:, :]),
                    waits=[t_ln2[i]])

    pro_drain()

    P.op("gpsimd", lambda e: e.nop(), waits=Qtok("gd"))
    P.op("sync", lambda e: e.nop(), waits=Qtok("ld"))

    with es:
        with nc.Block() as block:
            P.emit(block, sems)
    return nc


_program_cache = {}


def _get_program(T, nseq):
    key = (T, nseq)
    if key not in _program_cache:
        _program_cache[key] = build_program(T, nseq)
    return _program_cache[key]


def _consts():
    identb = np.eye(128, dtype=ml_dtypes.bfloat16)
    e0b = np.zeros((128, 128), dtype=ml_dtypes.bfloat16)
    e0b[0, :] = 1
    eqb = np.zeros((128, 512), dtype=ml_dtypes.bfloat16)
    for q in range(4):
        eqb[q, q * 128:(q + 1) * 128] = 1
    onesb = np.ones((128, 1), dtype=ml_dtypes.bfloat16)
    maskf = np.where(np.arange(128)[None, :] > np.arange(128)[:, None],
                     np.float32(NEG_BIG), np.float32(0.0)).astype(np.float32)
    return identb, e0b, eqb, onesb, maskf


def kernel(L0_attn_x, L0_attn_Wq, L0_attn_Wk, L0_attn_Wv,
           L0_ffn_w1, L0_ffn_conv_w, L0_ffn_conv_b, L0_ffn_w2):
    x = np.ascontiguousarray(np.asarray(L0_attn_x, dtype=np.float32))
    wq = np.ascontiguousarray(np.asarray(L0_attn_Wq, dtype=np.float32).T)
    wk = np.ascontiguousarray(np.asarray(L0_attn_Wk, dtype=np.float32).T)
    wv = np.ascontiguousarray(np.asarray(L0_attn_Wv, dtype=np.float32))
    w1t = np.ascontiguousarray(np.asarray(L0_ffn_w1, dtype=np.float32).T)
    w2t = np.ascontiguousarray(np.asarray(L0_ffn_w2, dtype=np.float32).T)
    cw = np.ascontiguousarray(
        np.asarray(L0_ffn_conv_w, dtype=np.float32).reshape(NH, 128, 3)
        .transpose(1, 0, 2).reshape(128, NH * 3))
    cb = np.ascontiguousarray(
        np.asarray(L0_ffn_conv_b, dtype=np.float32).reshape(NH, 128).T)
    identb, e0b, eqb, onesb, maskf = _consts()

    B, T, _ = x.shape
    nseq = B // N_CORES
    nc = _get_program(T, nseq)
    in_maps = []
    for c in range(N_CORES):
        in_maps.append({
            "x": np.ascontiguousarray(x[c * nseq:(c + 1) * nseq].reshape(nseq * T, D)),
            "wq": wq, "wk": wk, "wv": wv, "w1t": w1t, "w2t": w2t,
            "cw": cw, "cb": cb, "identb": identb, "e0b": e0b, "eqb": eqb,
            "onesb": onesb, "maskf": maskf,
        })
    res = run_bass_kernel_spmd(nc, in_maps, core_ids=list(range(N_CORES)))
    out = np.concatenate(
        [r["y"].reshape(nseq, T, D) for r in res.results], axis=0)
    return out.astype(np.float32)
